# version 05
# baseline (speedup 1.0000x reference)
import numpy as np
from contextlib import ExitStack

import concourse.bass as bass
import concourse.mybir as mybir
from concourse.bass_utils import run_bass_kernel_spmd
from concourse.alu_op_type import AluOpType as OP

F32 = mybir.dt.float32
BF16 = mybir.dt.bfloat16
I8 = mybir.dt.int8
I32 = mybir.dt.int32
AF = mybir.ActivationFunctionType
AX = mybir.AxisListType.X

B, T, C, DF = 16, 2048, 1024, 4096
NCORES = 8
SPER = B // NCORES
NT = 512
JTILES = T // NT
KC = C // 128
FC = DF // 128
NB = NT // 128
EPS_LN = 1e-5
EPS_Q = 1e-6
RND_C = 12582912.0

VECS = ["ln1_g", "ln1_b", "ln2_g", "ln2_b", "tm_k", "tm_v", "tm_r",
        "time_decay", "time_first", "fm_k", "fm_r", "wscale", "bias"]
VIDX = {n: i for i, n in enumerate(VECS)}

WNAMES = ["wkT", "wvT", "wrT", "woT", "fkT", "frT", "fvT", "wqT"]


class Sched:
    ENGS = ("v", "s", "t", "g", "y")

    def __init__(self):
        self.items = {e: [] for e in self.ENGS}
        self.cnt = {e: 0 for e in self.ENGS}
        self.hiw = {e: {f: 0 for f in self.ENGS} for e in self.ENGS}
        self.last_writer = {}
        self.readers = {}

    def op(self, eng, fn, reads=(), writes=(), dma=False, label=""):
        deps = []
        for r in reads:
            w = self.last_writer.get(r)
            if w is not None:
                deps.append(w)
        for r in writes:
            w = self.last_writer.get(r)
            if w is not None:
                deps.append(w)
            deps.extend(self.readers.get(r, ()))
        for (de, dv) in deps:
            if de == eng:
                continue
            if dv > self.hiw[eng][de]:
                self.hiw[eng][de] = dv
                self.items[eng].append(("wait", de, dv))
        inc = 16 if dma else 1
        self.cnt[eng] += inc
        tok = (eng, self.cnt[eng])
        self.items[eng].append(("op", fn, inc, label))
        for r in reads:
            self.readers.setdefault(r, []).append(tok)
        for r in writes:
            self.last_writer[r] = tok
            self.readers[r] = []
        return tok


class Pool:
    def __init__(self, total_cols):
        self.free = [(0, total_cols)]
        self.total = total_cols
        self.peak = 0

    def alloc(self, cols, align=128):
        for i, (off, ln) in enumerate(self.free):
            aoff = (off + align - 1) // align * align
            if aoff + cols <= off + ln:
                repl = [p for p in ((off, aoff - off),
                                    (aoff + cols, off + ln - aoff - cols)) if p[1] > 0]
                self.free[i:i + 1] = repl
                self.peak = max(self.peak, self.total - sum(l for _, l in self.free))
                return aoff
        raise RuntimeError(f"SBUF pool exhausted: want {cols}, free={self.free}")

    def release(self, off, cols):
        self.free.append((off, cols))
        self.free.sort()
        merged = []
        for seg in self.free:
            if merged and merged[-1][0] + merged[-1][1] == seg[0]:
                merged[-1] = (merged[-1][0], merged[-1][1] + seg[1])
            else:
                merged.append(seg)
        self.free = merged


class Buf:
    def __init__(self, arena, pool, cols, view_off=None):
        self.pool = pool
        self.cols = cols
        self.off = pool.alloc(cols) if view_off is None else view_off
        self.is_view = view_off is not None
        self.arena = arena
        self.ap_full = arena[:, self.off:self.off + cols]
        p0, p1 = self.off // 128, (self.off + cols + 127) // 128
        self.pages = tuple(f"sb{p}" for p in range(p0, p1))
        self.freed = False

    def ap(self, chunks=None):
        v = self.ap_full
        if chunks is not None:
            v = v.rearrange("p (c n) -> p c n", c=chunks)
        return v

    def free(self):
        assert not self.freed
        if not self.is_view:
            self.pool.release(self.off, self.cols)
        self.freed = True


def build_core_program(dev_tiles=None, dev_dump=None):
    nc = bass.Bass("TRN2", target_bir_lowering=False)

    x0h = nc.dram_tensor("x0", [SPER, C, T], F32, kind="ExternalInput").ap()
    outh = nc.dram_tensor("out", [SPER, C, T], F32, kind="ExternalOutput").ap()
    vech = nc.dram_tensor("vecs", [128, len(VECS) * KC], F32, kind="ExternalInput").ap()
    wh, wb = {}, {}
    for nm, shp, dt in [("wkT", [C, C], F32), ("wvT", [C, C], F32), ("wrT", [C, C], F32),
                        ("woT", [C, C], F32), ("fkT", [C, DF], F32), ("frT", [C, C], F32),
                        ("fvT", [DF, C], F32), ("wqT", [C, C], I8)]:
        wh[nm] = nc.dram_tensor(nm, shp, dt, kind="ExternalInput").ap()
        wb[nm] = nc.dram_tensor(nm + "_bf", shp, BF16, kind="Internal").ap()

    tiles = dev_tiles
    if tiles is None:
        tiles = [(s, j) for j in range(JTILES) for s in range(SPER)]

    sch = Sched()
    dumps = []

    with ExitStack() as ctx:
        ARENA = 28928
        arena_t = ctx.enter_context(nc.sbuf_tensor("arena", [128, ARENA], F32))
        arena = arena_t[:]
        pool = Pool(ARENA)

        NREG = 6
        bfa_t = ctx.enter_context(nc.sbuf_tensor("bfa", [128, NREG * 4096], BF16))
        bfa = bfa_t[:]

        RING_SLOTS, SLOT = 15, 1024
        ring_t = ctx.enter_context(nc.sbuf_tensor("ring", [128, RING_SLOTS * SLOT], BF16))
        ring = ring_t[:]

        vec_t = ctx.enter_context(nc.sbuf_tensor("vec", [128, len(VECS) * KC], F32))
        vec = vec_t[:]
        eww_t = ctx.enter_context(nc.sbuf_tensor("eww", [128, KC * 128], F32))
        eww = eww_t[:]
        dvec_t = ctx.enter_context(nc.sbuf_tensor("dvec", [128, 3 * KC], F32))
        dvec = dvec_t[:]
        eye_t = ctx.enter_context(nc.sbuf_tensor("eye", [128, 128], F32))
        eye = eye_t[:]
        ones1_t = ctx.enter_context(nc.sbuf_tensor("ones1", [1, 128], F32))
        onesb_t = ctx.enter_context(nc.sbuf_tensor("onesb", [128, 1], BF16))
        rows_t = ctx.enter_context(nc.sbuf_tensor("rows", [1, 2 * NT], F32))
        rows = rows_t[:]
        amx_t = ctx.enter_context(nc.sbuf_tensor("amx", [128, 3 * NB], F32))
        amx = amx_t[:]
        xss_t = ctx.enter_context(nc.sbuf_tensor("xss", [128, NT], F32))
        xss = xss_t[:]
        car_t = ctx.enter_context(nc.sbuf_tensor("car", [128, 4 * SPER * KC], F32))
        car = car_t[:]
        rsc_t = ctx.enter_context(nc.sbuf_tensor("rsc", [128, 2 * NT], F32))
        rsc = rsc_t[:]
        cst_t = ctx.enter_context(nc.sbuf_tensor("cst", [1, 4], F32))
        cst = cst_t[:]

        psA_t = ctx.enter_context(nc.psum_tensor("psA", [128, 4096], F32))
        psA = psA_t[:]

        def bank(i):
            return psA[:, i * NT:(i + 1) * NT]

        sems = {e: ctx.enter_context(nc.semaphore("sem_" + e)) for e in Sched.ENGS}
        block = ctx.enter_context(nc.Block())

        RB = [f"bank{i}" for i in range(8)]
        R_VEC, R_EWW, R_EYE, R_ROWS, R_AMX, R_XSS = "vec", "eww", "eye", "rows", "amx", "xss"

        def carres(s, w):
            return f"car_{s}_{w}"

        def carcol(which, s):
            base = {"a": 0, "b": SPER, "h": 2 * SPER, "g": 3 * SPER}[which]
            return car[:, (base + s) * KC:(base + s + 1) * KC]

        def bfpages(lo, hi):
            return tuple(f"bf{p}" for p in range(lo // 512, (hi + 511) // 512))

        def reg(i, kc=None):
            if kc is None:
                return bfa[:, i * 4096:(i + 1) * 4096]
            return bfa[:, i * 4096 + kc * NT: i * 4096 + (kc + 1) * NT]

        def regpages(i):
            return bfpages(i * 4096, (i + 1) * 4096)

        def ringres(slot):
            return f"ring{slot}"

        ring_state = {"next": 0}

        def ring_alloc(nslots):
            s0 = ring_state["next"]
            if s0 + nslots > RING_SLOTS:
                s0 = 0
            ring_state["next"] = s0 + nslots
            return s0

        def vcol(name, kc):
            i = VIDX[name] * KC + kc
            return vec[:, i:i + 1]

        def new16():
            return Buf(arena, pool, KC * NT)

        H_STRIDE = NT + 16

        def newH():
            return Buf(arena, pool, KC * H_STRIDE)

        for nm in WNAMES:
            def stg(g, nm=nm):
                g.dma_start(out=wb[nm], in_=wh[nm])
            sch.op("g", stg, reads=(), writes=(f"hbm_{nm}",), dma=True, label=f"stage_{nm}")

        def ld_vec(y):
            y.dma_start(out=vec, in_=vech)
        sch.op("y", ld_vec, writes=(R_VEC,), dma=True)

        iobuf = Buf(arena, pool, 256)
        ioa_ap = iobuf.ap_full[:, 0:128].bitcast(I32)
        iob_ap = iobuf.ap_full[:, 128:256].bitcast(I32)
        def io1(g):
            g.iota(ioa_ap, pattern=[[1, 128]], channel_multiplier=0)
        def io2(g):
            g.iota(iob_ap, pattern=[[0, 128]], channel_multiplier=1)
        sch.op("g", io1, writes=iobuf.pages)
        sch.op("g", io2, writes=iobuf.pages)
        def mkeye(v):
            v.tensor_tensor(eye, ioa_ap, iob_ap, OP.is_equal)
        sch.op("v", mkeye, reads=iobuf.pages, writes=(R_EYE,))
        iobuf.free()
        def mk1(v):
            v.memset(ones1_t[:], 1.0)
        sch.op("v", mk1, writes=("ones1",))
        def mk2(v):
            v.memset(onesb_t[:], 1.0)
        sch.op("v", mk2, writes=("onesb",))
        def ew1(s):
            s.activation(dvec[:, 0:KC], vec[:, VIDX["time_decay"] * KC:(VIDX["time_decay"] + 1) * KC], AF.Exp)
        sch.op("s", ew1, reads=(R_VEC,), writes=("dvec",))
        def ew2(s):
            s.activation(dvec[:, KC:2 * KC], dvec[:, 0:KC], AF.Exp, scale=-1.0)
        sch.op("s", ew2, reads=("dvec",), writes=("dvec",))
        def ew3(s):
            s.activation(dvec[:, 2 * KC:3 * KC],
                         vec[:, VIDX["time_first"] * KC:(VIDX["time_first"] + 1) * KC], AF.Exp)
        sch.op("s", ew3, reads=(R_VEC,), writes=("dvec",))
        def eww0(v):
            v.memset(eww, 1.0)
        sch.op("v", eww0, writes=(R_EWW,))
        def mkcst(v):
            v.memset(cst[0:1, 0:1], EPS_LN)
        sch.op("v", mkcst, writes=("cst",))
        for kc in range(KC):
            def ewwk(v, kc=kc):
                v.tensor_scalar(eww[:, kc * 128:(kc + 1) * 128], eww[:, kc * 128:(kc + 1) * 128],
                                dvec[:, KC + kc:KC + kc + 1], None, OP.mult)
            sch.op("v", ewwk, reads=("dvec", R_EWW), writes=(R_EWW,))

        def stream_one(nm, mc, kchunks, lb):
            src = wb[nm]
            nsl = max(1, (kchunks * 128) // SLOT)
            s0 = ring_alloc(nsl)
            dst3 = ring[:, s0 * SLOT:s0 * SLOT + kchunks * 128] \
                .rearrange("p (k m) -> p k m", k=kchunks)
            src3 = src[:, mc * 128:(mc + 1) * 128].rearrange("(k p) m -> p k m", p=128)
            rres = tuple(ringres(s0 + i) for i in range(nsl))
            def fn(y, dst3=dst3, src3=src3):
                y.dma_start(out=dst3, in_=src3)
            sch.op("y", fn, reads=(f"hbm_{nm}",), writes=rres, dma=True, label=f"{lb}_w{nm}{mc}")
            return (s0, nsl, rres, mc)

        def slab_block(s0, kc):
            return ring[:, s0 * SLOT + kc * 128: s0 * SLOT + (kc + 1) * 128]

        bank_rr = {"i": 2}

        def next_bank():
            b = bank_rr["i"]
            bank_rr["i"] = 2 + (b - 2 + 1) % 4
            return b

        ln_bp = {"i": 0}

        def next_lnpair():
            p = ((0, 1), (6, 7))[ln_bp["i"] % 2]
            ln_bp["i"] += 1
            return p

        def mm_group(bk, slab, rhs_fn, nK, extra_reads=(), label=""):
            s0, nsl, rres, mc = slab
            def fn(t):
                for kcc in range(nK):
                    t.matmul(bank(bk), slab_block(s0, kcc), rhs_fn(kcc),
                             start=(kcc == 0), stop=(kcc == nK - 1))
            return sch.op("t", fn, reads=rres + tuple(extra_reads), writes=(RB[bk],), label=label)

        def ln_stats(xbuf, lb, bp, r1=5, r2=5):
            b0, b1 = bp
            def cast(s):
                s.activation(reg(r1), xbuf.ap(), AF.Copy)
            sch.op("s", cast, reads=xbuf.pages, writes=regpages(r1), label=lb + "_cast")
            def mm1(t):
                for kc in range(KC):
                    t.matmul(psA[0:1, b0 * NT:b0 * NT + NT], onesb_t[:], reg(r1, kc),
                             start=(kc == 0), stop=(kc == KC - 1))
            sch.op("t", mm1, reads=regpages(r1) + ("onesb",), writes=(RB[b0],), label=lb + "_S1")
            def sq(s):
                s.activation(reg(r2), xbuf.ap(), AF.Square)
            sch.op("s", sq, reads=xbuf.pages, writes=regpages(r2), label=lb + "_sq")
            def mm2(t):
                for kc in range(KC):
                    t.matmul(psA[0:1, b1 * NT:b1 * NT + NT], onesb_t[:], reg(r2, kc),
                             start=(kc == 0), stop=(kc == KC - 1))
            sch.op("t", mm2, reads=regpages(r2) + ("onesb",), writes=(RB[b1],), label=lb + "_S2")
            def rm1(v):
                v.tensor_scalar(rows[0:1, 0:NT], psA[0:1, b0 * NT:b0 * NT + NT], 1.0 / C, None, OP.mult)
            sch.op("v", rm1, reads=(RB[b0],), writes=(R_ROWS,), label=lb + "_m")
            def rm2(v):
                v.tensor_tensor(rows[0:1, NT:2 * NT], rows[0:1, 0:NT], rows[0:1, 0:NT], OP.mult)
            sch.op("v", rm2, reads=(R_ROWS,), writes=(R_ROWS,))
            def rm3(v):
                v.scalar_tensor_tensor(rows[0:1, NT:2 * NT], psA[0:1, b1 * NT:b1 * NT + NT], 1.0 / C,
                                       rows[0:1, NT:2 * NT], OP.mult, OP.subtract)
            sch.op("v", rm3, reads=(RB[b1], R_ROWS), writes=(R_ROWS,), label=lb + "_var")
            def rm4(s):
                s.activation(rows[0:1, NT:2 * NT], rows[0:1, NT:2 * NT], AF.Ln, bias=cst[0:1, 0:1])
            sch.op("s", rm4, reads=(R_ROWS, "cst"), writes=(R_ROWS,))
            def rm5(s):
                s.activation(rows[0:1, NT:2 * NT], rows[0:1, NT:2 * NT], AF.Exp, scale=-0.5)
            sch.op("s", rm5, reads=(R_ROWS,), writes=(R_ROWS,), label=lb + "_inv")
            def bcm(t):
                t.matmul(bank(b0), ones1_t[:], rows[0:1, 0:NT], start=True, stop=True)
            sch.op("t", bcm, reads=(R_ROWS, "ones1"), writes=(RB[b0],), label=lb + "_bcM")
            def bci(t):
                t.matmul(bank(b1), ones1_t[:], rows[0:1, NT:2 * NT], start=True, stop=True)
            sch.op("t", bci, reads=(R_ROWS, "ones1"), writes=(RB[b1],), label=lb + "_bcI")

        def dump(name, apv, pages):
            if dev_dump is None or name not in dev_dump:
                return
            shp = list(apv.shape)
            d = nc.dram_tensor("dump_" + name, shp, apv.dtype, kind="ExternalOutput").ap()
            def fn(y):
                y.dma_start(out=d, in_=apv)
            sch.op("y", fn, reads=tuple(pages), writes=(f"dump_{name}",), dma=True, label="dump_" + name)

        def ln_apply(xbuf, dstbuf, lb, bp):
            b0, b1 = bp
            m3 = bank(b0).unsqueeze(1).broadcast_to([128, KC, NT])
            i3 = bank(b1).unsqueeze(1).broadcast_to([128, KC, NT])
            def a1(v):
                v.tensor_tensor(dstbuf.ap(chunks=KC), xbuf.ap(chunks=KC), m3, OP.subtract)
            sch.op("v", a1, reads=xbuf.pages + (RB[b0],), writes=dstbuf.pages, label=lb + "_sub")
            def a2(v):
                v.tensor_tensor(dstbuf.ap(chunks=KC), dstbuf.ap(chunks=KC), i3, OP.mult)
            sch.op("v", a2, reads=dstbuf.pages + (RB[b1],), writes=dstbuf.pages, label=lb + "_mul")

        def tile_prog(ti, s, j):
            lb = f"t{ti}"
            X0A = new16()
            def ldx0(y, s=s, j=j, X0A=X0A):
                y.dma_start(out=X0A.ap(chunks=KC),
                            in_=x0h[s].rearrange("(k p) t -> p k t", p=128)[:, :, j * NT:(j + 1) * NT])
            sch.op("g", ldx0, writes=X0A.pages, dma=True, label=lb + "_ldx0")

            XA = new16()
            bp0 = next_lnpair()
            ln_stats(X0A, lb + "_ln0", bp0)
            yield
            ln_apply(X0A, XA, lb + "_ln0", bp0)
            X0A.free()
            dump(f"xa_{s}_{j}", XA.ap(), XA.pages)

            bp1 = next_lnpair()
            ln_stats(XA, lb + "_ln1", bp1)
            yield
            XH = new16()
            ln_apply(XA, XH, lb + "_ln1", bp1)

            H = newH()
            Hc = H.ap_full.rearrange("p (c n) -> p c n", c=KC)
            if j == 0:
                def h0(v, Hc=Hc):
                    v.memset(Hc[:, :, 0:1], 0.0)
                sch.op("v", h0, writes=H.pages, label=lb + "_h0")
            else:
                def hc(v, Hc=Hc, s=s):
                    v.tensor_copy(Hc[:, :, 0:1], carcol("h", s).unsqueeze(2))
                sch.op("v", hc, reads=(carres(s, "h"),), writes=H.pages, label=lb + "_hc")
            for kc in range(KC):
                def aff(v, kc=kc, XH=XH, Hc=Hc):
                    v.tensor_scalar(Hc[:, kc, 1:NT + 1], XH.ap(chunks=KC)[:, kc, :],
                                    vcol("ln1_g", kc), vcol("ln1_b", kc), OP.mult, OP.add)
                sch.op("v", aff, reads=XH.pages + (R_VEC,), writes=H.pages, label=lb + f"_h{kc}")
            dump(f"h_{s}_{j}", H.ap_full, H.pages)

            D = XH
            def mkd(v, D=D, Hc=Hc):
                v.tensor_tensor(D.ap(chunks=KC), Hc[:, :, 1:NT + 1], Hc[:, :, 0:NT], OP.subtract)
            sch.op("v", mkd, reads=H.pages, writes=D.pages, label=lb + "_d")
            def hs(v, Hc=Hc, s=s):
                v.tensor_copy(carcol("h", s), Hc[:, :, NT:NT + 1].squeeze(2))
            sch.op("v", hs, reads=H.pages, writes=(carres(s, "h"),), label=lb + "_hs")

            for ri, vn in [(2, "tm_k"), (3, "tm_v"), (4, "tm_r")]:
                for kc in range(KC):
                    def mx(v, ri=ri, kc=kc, vn=vn, D=D, Hc=Hc):
                        v.scalar_tensor_tensor(reg(ri, kc), D.ap(chunks=KC)[:, kc, :], vcol(vn, kc),
                                               Hc[:, kc, 0:NT], OP.mult, OP.add)
                    sch.op("v", mx, reads=D.pages + H.pages + (R_VEC,),
                           writes=bfpages(ri * 4096 + kc * NT, ri * 4096 + (kc + 1) * NT),
                           label=lb + f"_mx{ri}{kc}")
            D.free()
            yield

            EK, VV, SR = new16(), new16(), new16()
            first_mat = True
            for nm, ri in [("wkT", 2), ("wvT", 3), ("wrT", 4)]:
                if not first_mat:
                    yield
                first_mat = False
                for mc in range(KC):
                    sl = stream_one(nm, mc, KC, lb)
                    bk = next_bank()
                    mm_group(bk, sl, lambda kcc, ri=ri: reg(ri, kcc), KC,
                             extra_reads=regpages(ri), label=lb + f"_{nm}{mc}")
                    if nm == "wkT":
                        def e1(sc, bk=bk, mc=mc, EK=EK):
                            sc.activation(EK.ap(chunks=KC)[:, mc, :], bank(bk), AF.Exp)
                        sch.op("s", e1, reads=(RB[bk],), writes=EK.pages, label=lb + f"_ek{mc}")
                    elif nm == "wvT":
                        def e3(sc, bk=bk, mc=mc, VV=VV):
                            sc.activation(VV.ap(chunks=KC)[:, mc, :], bank(bk), AF.Copy)
                        sch.op("s", e3, reads=(RB[bk],), writes=VV.pages, label=lb + f"_v{mc}")
                    else:
                        def e4(sc, bk=bk, mc=mc, SR=SR):
                            sc.activation(SR.ap(chunks=KC)[:, mc, :], bank(bk), AF.Copy)
                        sch.op("s", e4, reads=(RB[bk],), writes=SR.pages, label=lb + f"_sr{mc}")
            dump(f"ek_{s}_{j}", EK.ap(), EK.pages)
            dump(f"vv_{s}_{j}", VV.ap(), VV.pages)
            dump(f"sr_{s}_{j}", SR.ap(), SR.pages)
            yield

            def scan_into(dst, databuf, cw):
                d3 = dst.ap_full.rearrange("p (c n) -> p c n", c=KC)
                if j == 0:
                    def c0(v, d3=d3):
                        v.memset(d3[:, :, 0:1], 0.0)
                    sch.op("v", c0, writes=dst.pages, label=lb + f"_{cw}c0")
                else:
                    def c1(v, d3=d3, cw=cw, s=s):
                        v.tensor_copy(d3[:, :, 0:1], carcol(cw, s).unsqueeze(2))
                    sch.op("v", c1, reads=(carres(s, cw),), writes=dst.pages, label=lb + f"_{cw}c")
                for q in range(NB):
                    for kc in range(KC):
                        if q == 0:
                            init = (0.0 if j == 0 else carcol(cw, s)[:, kc:kc + 1])
                            ird = () if j == 0 else (carres(s, cw),)
                        else:
                            init = d3[:, kc, q * 128:q * 128 + 1]
                            ird = dst.pages
                        def scn(v, d3=d3, kc=kc, q=q, databuf=databuf, init=init):
                            v.tensor_tensor_scan(d3[:, kc, 1 + q * 128:1 + (q + 1) * 128],
                                                 eww[:, kc * 128:(kc + 1) * 128],
                                                 databuf.ap(chunks=KC)[:, kc, q * 128:(q + 1) * 128],
                                                 init, OP.mult, OP.add)
                        sch.op("v", scn, reads=databuf.pages + (R_EWW,) + ird, writes=dst.pages,
                               label=lb + f"_sc{cw}{kc}_{q}")

            def save_carry(dst, cw):
                d3 = dst.ap_full.rearrange("p (c n) -> p c n", c=KC)
                def csv(v, d3=d3, cw=cw, s=s):
                    v.tensor_copy(carcol(cw, s), d3[:, :, NT:NT + 1].squeeze(2))
                sch.op("v", csv, reads=dst.pages, writes=(carres(s, cw),), label=lb + f"_csv{cw}")

            HB = Buf(arena, pool, KC * H_STRIDE, view_off=H.off)
            scan_into(HB, EK, "b")
            def mekv(v, EK=EK, VV=VV):
                v.tensor_tensor(VV.ap(), EK.ap(), VV.ap(), OP.mult)
            sch.op("v", mekv, reads=EK.pages + VV.pages, writes=VV.pages, label=lb + "_ekv")
            save_carry(HB, "b")
            Hsh = HB.ap_full.rearrange("p (c n) -> p c n", c=KC)
            for kc in range(KC):
                def mden(v, kc=kc, EK=EK, Hsh=Hsh):
                    v.scalar_tensor_tensor(EK.ap(chunks=KC)[:, kc, :], EK.ap(chunks=KC)[:, kc, :],
                                           dvec[:, 2 * KC + kc:2 * KC + kc + 1],
                                           Hsh[:, kc, 0:NT], OP.mult, OP.add)
                sch.op("v", mden, reads=HB.pages + EK.pages + ("dvec",), writes=EK.pages,
                       label=lb + f"_den{kc}")
            DEN = EK
            dump(f"den_{s}_{j}", DEN.ap(), DEN.pages)
            yield
            scan_into(HB, VV, "a")
            dump(f"A_{s}_{j}", HB.ap_full, HB.pages)
            for kc in range(KC):
                def mnum(v, kc=kc, VV=VV, Hsh=Hsh):
                    v.scalar_tensor_tensor(VV.ap(chunks=KC)[:, kc, :], VV.ap(chunks=KC)[:, kc, :],
                                           dvec[:, 2 * KC + kc:2 * KC + kc + 1],
                                           Hsh[:, kc, 0:NT], OP.mult, OP.add)
                sch.op("v", mnum, reads=HB.pages + VV.pages + ("dvec",), writes=VV.pages,
                       label=lb + f"_num{kc}")
            NUM = VV
            dump(f"num_{s}_{j}", NUM.ap(), NUM.pages)
            save_carry(HB, "a")
            LND = Buf(arena, pool, KC * NT, view_off=H.off)
            def mlnd(sc, LND=LND, DEN=DEN):
                sc.activation(LND.ap(), DEN.ap(), AF.Ln)
            sch.op("s", mlnd, reads=DEN.pages, writes=LND.pages, label=lb + "_lnd")
            RDEN = DEN
            def mrd(sc, RDEN=RDEN, LND=LND):
                sc.activation(RDEN.ap(), LND.ap(), AF.Exp, scale=-1.0)
            sch.op("s", mrd, reads=LND.pages, writes=RDEN.pages, label=lb + "_rden")
            LND.free()
            HB.free()
            def srsig(sc, SR=SR):
                sc.activation(SR.ap(), SR.ap(), AF.Sigmoid)
            sch.op("s", srsig, reads=SR.pages, writes=SR.pages, label=lb + "_srsig")
            def mt1(v, NUM=NUM, RDEN=RDEN):
                v.tensor_tensor(NUM.ap(), NUM.ap(), RDEN.ap(), OP.mult)
            sch.op("v", mt1, reads=NUM.pages + RDEN.pages, writes=NUM.pages, label=lb + "_t1")
            T1 = NUM
            dump(f"wkv_{s}_{j}", T1.ap(), T1.pages)
            def mrw(v, T1=T1, SR=SR):
                v.tensor_tensor(reg(5), T1.ap(), SR.ap(), OP.mult)
            sch.op("v", mrw, reads=T1.pages + SR.pages, writes=regpages(5), label=lb + "_rwkv")
            EK.free()
            VV.free()
            SR.free()
            yield

            X1 = new16()
            for mc in range(KC):
                if mc == 4:
                    yield
                sl = stream_one("woT", mc, KC, lb)
                bk = next_bank()
                mm_group(bk, sl, lambda kcc: reg(5, kcc), KC, extra_reads=regpages(5),
                         label=lb + f"_wo{mc}")
                def mx1(v, bk=bk, mc=mc, X1=X1, XA=XA):
                    v.tensor_tensor(X1.ap(chunks=KC)[:, mc, :], XA.ap(chunks=KC)[:, mc, :],
                                    bank(bk), OP.add)
                sch.op("v", mx1, reads=(RB[bk],) + XA.pages, writes=X1.pages, label=lb + f"_x1{mc}")
            XA.free()
            dump(f"x1_{s}_{j}", X1.ap(), X1.pages)
            yield

            bp2 = next_lnpair()
            ln_stats(X1, lb + "_ln2", bp2)
            yield
            XH2 = new16()
            ln_apply(X1, XH2, lb + "_ln2", bp2)
            G = H
            Gc = G.ap_full.rearrange("p (c n) -> p c n", c=KC)
            if j == 0:
                def g0(v, Gc=Gc):
                    v.memset(Gc[:, :, 0:1], 0.0)
                sch.op("v", g0, writes=G.pages, label=lb + "_g0")
            else:
                def gc(v, Gc=Gc, s=s):
                    v.tensor_copy(Gc[:, :, 0:1], carcol("g", s).unsqueeze(2))
                sch.op("v", gc, reads=(carres(s, "g"),), writes=G.pages, label=lb + "_gc")
            for kc in range(KC):
                def aff2(v, kc=kc, XH2=XH2, Gc=Gc):
                    v.tensor_scalar(Gc[:, kc, 1:NT + 1], XH2.ap(chunks=KC)[:, kc, :],
                                    vcol("ln2_g", kc), vcol("ln2_b", kc), OP.mult, OP.add)
                sch.op("v", aff2, reads=XH2.pages + (R_VEC,), writes=G.pages, label=lb + f"_g{kc}")
            D2 = XH2
            def mkd2(v, D2=D2, Gc=Gc):
                v.tensor_tensor(D2.ap(chunks=KC), Gc[:, :, 1:NT + 1], Gc[:, :, 0:NT], OP.subtract)
            sch.op("v", mkd2, reads=G.pages, writes=D2.pages, label=lb + "_d2")
            def gs(v, Gc=Gc, s=s):
                v.tensor_copy(carcol("g", s), Gc[:, :, NT:NT + 1].squeeze(2))
            sch.op("v", gs, reads=G.pages, writes=(carres(s, "g"),), label=lb + "_gs")
            for ri, vn in [(0, "fm_k"), (1, "fm_r")]:
                for kc in range(KC):
                    def mx2(v, ri=ri, kc=kc, vn=vn, D2=D2, Gc=Gc):
                        v.scalar_tensor_tensor(reg(ri, kc), D2.ap(chunks=KC)[:, kc, :], vcol(vn, kc),
                                               Gc[:, kc, 0:NT], OP.mult, OP.add)
                    sch.op("v", mx2, reads=D2.pages + G.pages + (R_VEC,),
                           writes=bfpages(ri * 4096 + kc * NT, ri * 4096 + (kc + 1) * NT),
                           label=lb + f"_mg{ri}{kc}")
            D2.free()
            H.free()
            yield

            SIG = new16()
            for mc in range(KC):
                sl = stream_one("frT", mc, KC, lb)
                bk = next_bank()
                mm_group(bk, sl, lambda kcc: reg(1, kcc), KC, extra_reads=regpages(1),
                         label=lb + f"_fr{mc}")
                def sg(sc, bk=bk, mc=mc, SIG=SIG):
                    sc.activation(SIG.ap(chunks=KC)[:, mc, :], bank(bk), AF.Sigmoid)
                sch.op("s", sg, reads=(RB[bk],), writes=SIG.pages, label=lb + f"_sig{mc}")

            yield
            kf_lo = 1 * 4096
            for mc in range(FC):
                if mc == 16:
                    yield
                sl = stream_one("fkT", mc, KC, lb)
                bk = next_bank()
                mm_group(bk, sl, lambda kcc: reg(0, kcc), KC, extra_reads=regpages(0),
                         label=lb + f"_fk{mc}")
                rs = rsc[:, (mc % 2) * NT:(mc % 2 + 1) * NT]
                def rl(sc, bk=bk, rs=rs):
                    sc.activation(rs, bank(bk), AF.Relu)
                sch.op("s", rl, reads=(RB[bk],), writes=(f"rsc{mc % 2}",), label=lb + f"_rl{mc}")
                dst = bfa[:, kf_lo + mc * NT: kf_lo + (mc + 1) * NT]
                def sq2(sc, rs=rs, dst=dst):
                    sc.activation(dst, rs, AF.Square)
                sch.op("s", sq2, reads=(f"rsc{mc % 2}",),
                       writes=bfpages(kf_lo + mc * NT, kf_lo + (mc + 1) * NT), label=lb + f"_kf{mc}")

            yield
            X2 = X1
            for mc in range(KC):
                halves = []
                for half in range(2):
                    s0 = ring_alloc(2)
                    dst3 = ring[:, s0 * SLOT:(s0 + 2) * SLOT].rearrange("p (k m) -> p k m", k=16)
                    src3 = wb["fvT"][half * 2048:(half + 1) * 2048, mc * 128:(mc + 1) * 128] \
                        .rearrange("(k p) m -> p k m", p=128)
                    rres = (ringres(s0), ringres(s0 + 1))
                    def fnl(y, dst3=dst3, src3=src3):
                        y.dma_start(out=dst3, in_=src3)
                    sch.op("y", fnl, reads=("hbm_fvT",), writes=rres, dma=True,
                           label=lb + f"_wfv{mc}_{half}")
                    halves.append((s0, rres))
                bk = next_bank()
                for half, (s0, rres) in enumerate(halves):
                    def fvmm(t, s0=s0, half=half, bk=bk):
                        for kcc in range(16):
                            fcc = half * 16 + kcc
                            t.matmul(bank(bk), slab_block(s0, kcc),
                                     bfa[:, kf_lo + fcc * NT: kf_lo + (fcc + 1) * NT],
                                     start=(fcc == 0), stop=(fcc == FC - 1))
                    sch.op("t", fvmm, reads=rres + bfpages(kf_lo, kf_lo + FC * NT),
                           writes=(RB[bk],), label=lb + f"_fv{mc}_{half}")
                def skv(v, bk=bk, mc=mc, SIG=SIG):
                    v.tensor_tensor(SIG.ap(chunks=KC)[:, mc, :], SIG.ap(chunks=KC)[:, mc, :],
                                    bank(bk), OP.mult)
                sch.op("v", skv, reads=(RB[bk],) + SIG.pages, writes=SIG.pages, label=lb + f"_skv{mc}")
                def mx2b(v, mc=mc, X2=X2, SIG=SIG):
                    v.tensor_tensor(X2.ap(chunks=KC)[:, mc, :], X2.ap(chunks=KC)[:, mc, :],
                                    SIG.ap(chunks=KC)[:, mc, :], OP.add)
                sch.op("v", mx2b, reads=X2.pages + SIG.pages, writes=X2.pages, label=lb + f"_x2{mc}")
            SIG.free()
            yield

            X0D = new16()
            def ldx0d(y, s=s, j=j, X0D=X0D):
                y.dma_start(out=X0D.ap(chunks=KC),
                            in_=x0h[s].rearrange("(k p) t -> p k t", p=128)[:, :, j * NT:(j + 1) * NT])
            sch.op("g", ldx0d, writes=X0D.pages, dma=True, label=lb + "_ldx0d")
            XX = X0D
            def mX(v, XX=XX, X2=X2):
                v.tensor_tensor(XX.ap(), XX.ap(), X2.ap(), OP.add)
            sch.op("v", mX, reads=XX.pages + X2.pages, writes=XX.pages, label=lb + "_X")
            dump(f"X_{s}_{j}", XX.ap(), XX.pages)
            X2.free()

            SQ3 = new16()
            def sq3(sc, XX=XX, SQ3=SQ3):
                sc.activation(SQ3.ap(), XX.ap(), AF.Square)
            sch.op("s", sq3, reads=XX.pages, writes=SQ3.pages, label=lb + "_sq3")
            bp3 = next_lnpair()
            qp = (6, 7) if bp3 == (0, 1) else (0, 1)
            b0, b1 = bp3
            def c3(sc, XX=XX):
                sc.activation(reg(3), XX.ap(), AF.Copy)
            sch.op("s", c3, reads=XX.pages, writes=regpages(3), label=lb + "_c3")
            def c3b(v, SQ3=SQ3):
                v.tensor_copy(reg(4), SQ3.ap())
            sch.op("v", c3b, reads=SQ3.pages, writes=regpages(4), label=lb + "_c3b")
            def mm1(t, b0=b0):
                for kc in range(KC):
                    t.matmul(psA[0:1, b0 * NT:b0 * NT + NT], onesb_t[:], reg(3, kc),
                             start=(kc == 0), stop=(kc == KC - 1))
            sch.op("t", mm1, reads=regpages(3) + ("onesb",), writes=(RB[b0],), label=lb + "_l3S1")
            def mm2(t, b1=b1):
                for kc in range(KC):
                    t.matmul(psA[0:1, b1 * NT:b1 * NT + NT], onesb_t[:], reg(4, kc),
                             start=(kc == 0), stop=(kc == KC - 1))
            sch.op("t", mm2, reads=regpages(4) + ("onesb",), writes=(RB[b1],), label=lb + "_l3S2")
            def rm1(v, b0=b0):
                v.tensor_scalar(rows[0:1, 0:NT], psA[0:1, b0 * NT:b0 * NT + NT], 1.0 / C, None, OP.mult)
            sch.op("v", rm1, reads=(RB[b0],), writes=(R_ROWS,))
            def rm2(v):
                v.tensor_tensor(rows[0:1, NT:2 * NT], rows[0:1, 0:NT], rows[0:1, 0:NT], OP.mult)
            sch.op("v", rm2, reads=(R_ROWS,), writes=(R_ROWS,))
            def rm3(v, b1=b1):
                v.scalar_tensor_tensor(rows[0:1, NT:2 * NT], psA[0:1, b1 * NT:b1 * NT + NT], 1.0 / C,
                                       rows[0:1, NT:2 * NT], OP.mult, OP.subtract)
            sch.op("v", rm3, reads=(RB[b1], R_ROWS), writes=(R_ROWS,))
            def rm4(sc):
                sc.activation(rows[0:1, NT:2 * NT], rows[0:1, NT:2 * NT], AF.Ln, bias=cst[0:1, 0:1])
            sch.op("s", rm4, reads=(R_ROWS, "cst"), writes=(R_ROWS,))
            def rm5(sc):
                sc.activation(rows[0:1, NT:2 * NT], rows[0:1, NT:2 * NT], AF.Exp, scale=-0.5)
            sch.op("s", rm5, reads=(R_ROWS,), writes=(R_ROWS,))
            def bcm(t, b0=b0):
                t.matmul(bank(b0), ones1_t[:], rows[0:1, 0:NT], start=True, stop=True)
            sch.op("t", bcm, reads=(R_ROWS, "ones1"), writes=(RB[b0],))
            def bci(t, b1=b1):
                t.matmul(bank(b1), ones1_t[:], rows[0:1, NT:2 * NT], start=True, stop=True)
            sch.op("t", bci, reads=(R_ROWS, "ones1"), writes=(RB[b1],))
            yield

            XF = new16()
            ln_apply(XX, XF, lb + "_ln3", bp3)
            dump(f"xf_{s}_{j}", XF.ap(), XF.pages)
            def sqf(sc, XF=XF, SQ3=SQ3):
                sc.activation(SQ3.ap(), XF.ap(), AF.Square)
            sch.op("s", sqf, reads=XF.pages, writes=SQ3.pages, label=lb + "_sqf")

            def t1f(v, SQ3=SQ3):
                v.tensor_tensor(SQ3.ap(chunks=KC)[:, 0:4, :], SQ3.ap(chunks=KC)[:, 0:4, :],
                                SQ3.ap(chunks=KC)[:, 4:8, :], OP.max)
            sch.op("v", t1f, reads=SQ3.pages, writes=SQ3.pages, label=lb + "_tr1")
            def t2f(v, SQ3=SQ3):
                v.tensor_tensor(SQ3.ap(chunks=KC)[:, 4:6, :], SQ3.ap(chunks=KC)[:, 0:2, :],
                                SQ3.ap(chunks=KC)[:, 2:4, :], OP.max)
            sch.op("v", t2f, reads=SQ3.pages, writes=SQ3.pages)
            def t3f(v, SQ3=SQ3):
                v.tensor_tensor(SQ3.ap(chunks=KC)[:, 6:7, :], SQ3.ap(chunks=KC)[:, 4:5, :],
                                SQ3.ap(chunks=KC)[:, 5:6, :], OP.max)
            sch.op("v", t3f, reads=SQ3.pages, writes=SQ3.pages)
            for bi in range(NB):
                def trp(t, bi=bi, SQ3=SQ3):
                    t.transpose(psA[:, qp[0] * NT + bi * 128:qp[0] * NT + (bi + 1) * 128],
                                SQ3.ap(chunks=KC)[:, 6, bi * 128:(bi + 1) * 128], eye)
                sch.op("t", trp, reads=SQ3.pages + (R_EYE,), writes=(RB[qp[0]],), label=lb + f"_trp{bi}")
            def redm(v):
                v.tensor_reduce(amx[:, 0:NB], bank(qp[0]).rearrange("p (a b) -> p a b", a=NB),
                                AX, OP.max)
            sch.op("v", redm, reads=(RB[qp[0]],), writes=(R_AMX,), label=lb + "_red")
            SQ3.free()
            def am1(sc):
                sc.activation(amx[:, NB:2 * NB], amx[:, 0:NB], AF.Ln)
            sch.op("s", am1, reads=(R_AMX,), writes=(R_AMX,))
            def am2(sc):
                sc.activation(amx[:, NB:2 * NB], amx[:, NB:2 * NB], AF.Exp, scale=0.5)
            sch.op("s", am2, reads=(R_AMX,), writes=(R_AMX,))
            def am3(v):
                v.tensor_scalar(amx[:, NB:2 * NB], amx[:, NB:2 * NB], EPS_Q, 1.0 / 127.0,
                                OP.max, OP.mult)
            sch.op("v", am3, reads=(R_AMX,), writes=(R_AMX,), label=lb + "_xs")
            def sep(v):
                v.memset(rsc[:, 0:1], 0.0)
            sch.op("v", sep, writes=("rscsep",))
            def am4(v):
                v.reciprocal(amx[:, 2 * NB:3 * NB], amx[:, NB:2 * NB])
            sch.op("v", am4, reads=(R_AMX,), writes=(R_AMX,), label=lb + "_rq")
            dump(f"amx_{s}_{j}", amx, (R_AMX,))
            for bi in range(NB):
                def tx(t, bi=bi):
                    t.transpose(psA[0:1, qp[1] * NT + bi * 128:qp[1] * NT + (bi + 1) * 128],
                                amx[:, NB + bi:NB + bi + 1], eye)
                sch.op("t", tx, reads=(R_AMX, R_EYE), writes=(RB[qp[1]],), label=lb + f"_tx{bi}")
            for bi in range(NB):
                def tr(t, bi=bi):
                    t.transpose(psA[0:1, qp[0] * NT + bi * 128:qp[0] * NT + (bi + 1) * 128],
                                amx[:, 2 * NB + bi:2 * NB + bi + 1], eye)
                sch.op("t", tr, reads=(R_AMX, R_EYE), writes=(RB[qp[0]],), label=lb + f"_tr{bi}")
            def evx(sc):
                sc.activation(rows[0:1, 0:NT], psA[0:1, qp[1] * NT:qp[1] * NT + NT], AF.Copy)
            sch.op("s", evx, reads=(RB[qp[1]],), writes=(R_ROWS,), label=lb + "_evxs")
            def evr(sc):
                sc.activation(rows[0:1, NT:2 * NT], psA[0:1, qp[0] * NT:qp[0] * NT + NT], AF.Copy)
            sch.op("s", evr, reads=(RB[qp[0]],), writes=(R_ROWS,), label=lb + "_evr")
            dump(f"rows_{s}_{j}", rows, (R_ROWS,))
            def bcr(t, b0=b0):
                t.matmul(bank(b0), ones1_t[:], rows[0:1, NT:2 * NT], start=True, stop=True)
            sch.op("t", bcr, reads=(R_ROWS, "ones1"), writes=(RB[b0],), label=lb + "_bcr")
            def bcx(t, b1=b1):
                t.matmul(bank(b1), ones1_t[:], rows[0:1, 0:NT], start=True, stop=True)
            sch.op("t", bcx, reads=(R_ROWS, "ones1"), writes=(RB[b1],), label=lb + "_bcx")
            def evxs(sc, b1=b1):
                sc.activation(xss, bank(b1), AF.Copy)
            sch.op("s", evxs, reads=(RB[b1],), writes=(R_XSS,), label=lb + "_evb")
            dump(f"xss_{s}_{j}", xss, (R_XSS,))

            r3 = bank(b0).unsqueeze(1).broadcast_to([128, KC, NT])
            def q1(v, XF=XF):
                v.tensor_tensor(XF.ap(chunks=KC), XF.ap(chunks=KC), r3, OP.mult)
            sch.op("v", q1, reads=XF.pages + (RB[b0],), writes=XF.pages, label=lb + "_q1")
            def q2(v, XF=XF):
                v.tensor_scalar(reg(2), XF.ap(), RND_C, RND_C, OP.add, OP.subtract)
            sch.op("v", q2, reads=XF.pages, writes=regpages(2), label=lb + "_q2")
            XF.free()

            yield
            OUTB = new16()
            for mc in range(KC):
                sl = stream_one("wqT", mc, KC, lb)
                bk = next_bank()
                mm_group(bk, sl, lambda kcc: reg(2, kcc), KC, extra_reads=regpages(2),
                         label=lb + f"_wq{mc}")
                def dq(v, bk=bk, mc=mc, OUTB=OUTB):
                    v.scalar_tensor_tensor(OUTB.ap(chunks=KC)[:, mc, :], bank(bk), vcol("wscale", mc),
                                           xss, OP.mult, OP.mult)
                sch.op("v", dq, reads=(RB[bk], R_XSS, R_VEC), writes=OUTB.pages, label=lb + f"_dq{mc}")
                def fin(v, mc=mc, OUTB=OUTB, XX=XX):
                    v.scalar_tensor_tensor(OUTB.ap(chunks=KC)[:, mc, :], OUTB.ap(chunks=KC)[:, mc, :],
                                           vcol("bias", mc), XX.ap(chunks=KC)[:, mc, :], OP.add, OP.add)
                sch.op("v", fin, reads=OUTB.pages + XX.pages + (R_VEC,), writes=OUTB.pages,
                       label=lb + f"_fin{mc}")
            XX.free()
            def store(y, s=s, j=j, OUTB=OUTB):
                y.dma_start(out=outh[s].rearrange("(k p) t -> p k t", p=128)[:, :, j * NT:(j + 1) * NT],
                            in_=OUTB.ap(chunks=KC))
            sch.op("g", store, reads=OUTB.pages, writes=(f"out_{s}_{j}",), dma=True, label=lb + "_store")
            dump(f"car_{s}_{j}", car, (carres(s, "a"), carres(s, "b"), carres(s, "h"), carres(s, "g")))
            OUTB.free()

        n_tiles = len(tiles)
        gens = {}

        def step(r, k=1):
            for _ in range(k):
                next(gens[r], None)

        for r in range(n_tiles + 1):
            if r < n_tiles:
                gens[r] = tile_prog(r, *tiles[r])
            p, q = r - 1, r
            if p >= 0:
                step(p, 2)
            if q < n_tiles:
                step(q)
            if p >= 0:
                step(p)
            if q < n_tiles:
                step(q)
            if p >= 0:
                step(p)
            if q < n_tiles:
                step(q)
            if p >= 0:
                step(p)
            if q < n_tiles:
                step(q)
            if p >= 0:
                step(p)
            if q < n_tiles:
                step(q, 2)
            if p >= 0:
                step(p, 2)
            if q < n_tiles:
                step(q)
            if p >= 0:
                step(p)
            if q < n_tiles:
                step(q)
            if p >= 0:
                step(p)
                for _ in gens[p]:
                    pass
                del gens[p]

        class IncWrap:
            def __init__(self, h):
                self._h = h
                self.last = None

            def __getattr__(self, k):
                a = getattr(self._h, k)
                if callable(a):
                    def call(*args, **kw):
                        r = a(*args, **kw)
                        if hasattr(r, "then_inc"):
                            self.last = r
                        return r
                    return call
                return a

        import os
        use_scopes = bool(os.environ.get("K_SCOPES"))

        def emit(eng, handle):
            w = IncWrap(handle)
            for item in sch.items[eng]:
                if item[0] == "wait":
                    _, de, dv = item
                    handle.wait_ge(sems[de], dv)
                else:
                    _, fn, inc, _lbl = item
                    w.last = None
                    if use_scopes:
                        with nc.named_scope(_lbl or "anon"):
                            fn(w)
                    else:
                        fn(w)
                    assert w.last is not None, f"op issued no instruction: {eng} {_lbl}"
                    w.last.then_inc(sems[eng], inc)

        @block.vector
        def _(v):
            emit("v", v)

        @block.scalar
        def _(s_):
            emit("s", s_)

        @block.tensor
        def _(t_):
            emit("t", t_)

        @block.gpsimd
        def _(g_):
            emit("g", g_)

        @block.sync
        def _(y_):
            emit("y", y_)
            y_.wait_ge(sems["y"], sch.cnt["y"])

    return nc


def _host_prep(inputs):
    x = np.asarray(inputs["L0_attn_x"], dtype=np.float32)
    xT = np.ascontiguousarray(np.transpose(x, (0, 2, 1)))
    tr = lambda k: np.ascontiguousarray(np.asarray(inputs[k], np.float32).T)
    shared = dict(
        wkT=tr("L0_attn_Wk"), wvT=tr("L0_attn_Wv"), wrT=tr("L0_attn_Wr"),
        woT=tr("L0_attn_Wo"), fkT=tr("L0_attn_Fk"), frT=tr("L0_attn_Fr"),
        fvT=tr("L0_attn_Fv"),
        wqT=np.ascontiguousarray(np.asarray(inputs["L0_ffn_weight_q"]).T),
    )
    vecmap = {
        "ln1_g": "L0_attn_ln1_g", "ln1_b": "L0_attn_ln1_b",
        "ln2_g": "L0_attn_ln2_g", "ln2_b": "L0_attn_ln2_b",
        "tm_k": "L0_attn_tm_k", "tm_v": "L0_attn_tm_v", "tm_r": "L0_attn_tm_r",
        "time_decay": "L0_attn_time_decay", "time_first": "L0_attn_time_first",
        "fm_k": "L0_attn_fm_k", "fm_r": "L0_attn_fm_r",
        "wscale": "L0_ffn_weight_scale", "bias": "L0_ffn_bias",
    }
    cols = [np.asarray(inputs[vecmap[nm]], np.float32).reshape(KC, 128).T for nm in VECS]
    shared["vecs"] = np.ascontiguousarray(np.concatenate(cols, axis=1))
    return xT, shared


def kernel(**inputs):
    xT, shared = _host_prep(inputs)
    nc = build_core_program()
    in_maps = []
    for ci in range(NCORES):
        m = dict(shared)
        m["x0"] = np.ascontiguousarray(xT[ci * SPER:(ci + 1) * SPER])
        in_maps.append(m)
    res = run_bass_kernel_spmd(nc, in_maps, core_ids=list(range(NCORES)))
    outs = [np.transpose(r["out"], (0, 2, 1)) for r in res.results]
    return np.ascontiguousarray(np.concatenate(outs, axis=0))


# version 06
# speedup vs baseline: 1.1183x; 1.1183x over previous
import numpy as np
from contextlib import ExitStack

import concourse.bass as bass
import concourse.mybir as mybir
from concourse.bass_utils import run_bass_kernel_spmd
from concourse.alu_op_type import AluOpType as OP

F32 = mybir.dt.float32
BF16 = mybir.dt.bfloat16
I8 = mybir.dt.int8
I32 = mybir.dt.int32
AF = mybir.ActivationFunctionType
AX = mybir.AxisListType.X

B, T, C, DF = 16, 2048, 1024, 4096
NCORES = 8
SPER = B // NCORES
NT = 512
JTILES = T // NT
KC = C // 128
FC = DF // 128
NB = NT // 128
EPS_LN = 1e-5
EPS_Q = 1e-6
RND_C = 12582912.0

VECS = ["ln1_g", "ln1_b", "ln2_g", "ln2_b", "tm_k", "tm_v", "tm_r",
        "time_decay", "time_first", "fm_k", "fm_r", "wscale", "bias"]
VIDX = {n: i for i, n in enumerate(VECS)}

WNAMES = ["wkT", "wvT", "wrT", "woT", "fkT", "frT", "fvT", "wqT"]


class Op:
    __slots__ = ("oid", "eng", "fn", "inc", "label", "cost", "deps", "hazard")

    def __init__(self, oid, eng, fn, inc, label, cost, deps, hazard):
        self.oid, self.eng, self.fn, self.inc = oid, eng, fn, inc
        self.label, self.cost, self.deps, self.hazard = label, cost, deps, hazard


class Sched:

    ENGS = ("v", "s", "t", "g", "y")
    DEF_COST = {"v": 700, "s": 700, "t": 500, "g": 2000, "y": 2000}

    def __init__(self):
        self.ops = []
        self.last_writer = {}
        self.readers = {}

    def op(self, eng, fn, reads=(), writes=(), dma=False, label="", cost=None, hazard=None):
        deps = set()
        for r in reads:
            w = self.last_writer.get(r)
            if w is not None:
                deps.add(w)
        for r in writes:
            w = self.last_writer.get(r)
            if w is not None:
                deps.add(w)
            deps.update(self.readers.get(r, ()))
        oid = len(self.ops)
        deps.discard(oid)
        inc = 16 if dma else 1
        if cost is None:
            cost = self.DEF_COST[eng]
        self.ops.append(Op(oid, eng, fn, inc, label, cost,
                           deps, hazard))
        for r in reads:
            self.readers.setdefault(r, []).append(oid)
        for r in writes:
            self.last_writer[r] = oid
            self.readers[r] = []
        return oid

    def schedule(self):
        import heapq
        n = len(self.ops)
        children = [[] for _ in range(n)]
        indeg = [0] * n
        for o in self.ops:
            for d in o.deps:
                children[d].append(o.oid)
                indeg[o.oid] += 1
        done = [0.0] * n
        placed = [False] * n
        eng_free = {e: 0.0 for e in self.ENGS}
        ready = {e: [] for e in self.ENGS}
        for o in self.ops:
            if indeg[o.oid] == 0:
                heapq.heappush(ready[o.eng], o.oid)
        order = {e: [] for e in self.ENGS}
        n_placed = 0
        while n_placed < n:
            best = None
            for e in self.ENGS:
                if not ready[e]:
                    continue
                oid = ready[e][0]
                o = self.ops[oid]
                start = eng_free[e]
                for d in o.deps:
                    if done[d] > start:
                        start = done[d]
                if best is None or (start, oid) < (best[0], best[1]):
                    best = (start, oid, e)
            start, oid, e = best
            heapq.heappop(ready[e])
            o = self.ops[oid]
            fin = start + o.cost
            done[oid] = fin
            eng_free[e] = fin
            placed[oid] = True
            order[e].append(oid)
            n_placed += 1
            for c in children[oid]:
                indeg[c] -= 1
                if indeg[c] == 0:
                    heapq.heappush(ready[self.ops[c].eng], c)
        self.order = order
        self.makespan = max(eng_free.values())
        return order


class Pool:
    def __init__(self, total_cols):
        self.free = [(0, total_cols)]
        self.total = total_cols
        self.peak = 0

    def alloc(self, cols, align=128):
        for i, (off, ln) in enumerate(self.free):
            aoff = (off + align - 1) // align * align
            if aoff + cols <= off + ln:
                repl = [p for p in ((off, aoff - off),
                                    (aoff + cols, off + ln - aoff - cols)) if p[1] > 0]
                self.free[i:i + 1] = repl
                self.peak = max(self.peak, self.total - sum(l for _, l in self.free))
                return aoff
        raise RuntimeError(f"SBUF pool exhausted: want {cols}, free={self.free}")

    def release(self, off, cols):
        self.free.append((off, cols))
        self.free.sort()
        merged = []
        for seg in self.free:
            if merged and merged[-1][0] + merged[-1][1] == seg[0]:
                merged[-1] = (merged[-1][0], merged[-1][1] + seg[1])
            else:
                merged.append(seg)
        self.free = merged


class Buf:
    def __init__(self, arena, pool, cols, view_off=None):
        self.pool = pool
        self.cols = cols
        self.off = pool.alloc(cols) if view_off is None else view_off
        self.is_view = view_off is not None
        self.arena = arena
        self.ap_full = arena[:, self.off:self.off + cols]
        p0, p1 = self.off // 128, (self.off + cols + 127) // 128
        self.pages = tuple(f"sb{p}" for p in range(p0, p1))
        self.freed = False

    def ap(self, chunks=None):
        v = self.ap_full
        if chunks is not None:
            v = v.rearrange("p (c n) -> p c n", c=chunks)
        return v

    def free(self):
        assert not self.freed
        if not self.is_view:
            self.pool.release(self.off, self.cols)
        self.freed = True


def build_core_program(dev_tiles=None, dev_dump=None):
    nc = bass.Bass("TRN2", target_bir_lowering=False)

    x0h = nc.dram_tensor("x0", [SPER, C, T], F32, kind="ExternalInput").ap()
    outh = nc.dram_tensor("out", [SPER, C, T], F32, kind="ExternalOutput").ap()
    vech = nc.dram_tensor("vecs", [128, len(VECS) * KC], F32, kind="ExternalInput").ap()
    wh, wb = {}, {}
    for nm, shp, dt in [("wkT", [C, C], F32), ("wvT", [C, C], F32), ("wrT", [C, C], F32),
                        ("woT", [C, C], F32), ("fkT", [C, DF], F32), ("frT", [C, C], F32),
                        ("fvT", [DF, C], F32), ("wqT", [C, C], I8)]:
        wh[nm] = nc.dram_tensor(nm, shp, dt, kind="ExternalInput").ap()
        wb[nm] = nc.dram_tensor(nm + "_bf", shp, BF16, kind="Internal").ap()

    tiles = dev_tiles
    if tiles is None:
        tiles = [(s, j) for j in range(JTILES) for s in range(SPER)]

    sch = Sched()
    dumps = []

    with ExitStack() as ctx:
        ARENA = 28928
        arena_t = ctx.enter_context(nc.sbuf_tensor("arena", [128, ARENA], F32))
        arena = arena_t[:]
        pool = Pool(ARENA)

        NREG = 6
        bfa_t = ctx.enter_context(nc.sbuf_tensor("bfa", [128, NREG * 4096], BF16))
        bfa = bfa_t[:]

        RING_SLOTS, SLOT = 15, 1024
        ring_t = ctx.enter_context(nc.sbuf_tensor("ring", [128, RING_SLOTS * SLOT], BF16))
        ring = ring_t[:]

        vec_t = ctx.enter_context(nc.sbuf_tensor("vec", [128, len(VECS) * KC], F32))
        vec = vec_t[:]
        eww_t = ctx.enter_context(nc.sbuf_tensor("eww", [128, KC * 128], F32))
        eww = eww_t[:]
        dvec_t = ctx.enter_context(nc.sbuf_tensor("dvec", [128, 3 * KC], F32))
        dvec = dvec_t[:]
        eye_t = ctx.enter_context(nc.sbuf_tensor("eye", [128, 128], F32))
        eye = eye_t[:]
        ones1_t = ctx.enter_context(nc.sbuf_tensor("ones1", [1, 128], F32))
        onesb_t = ctx.enter_context(nc.sbuf_tensor("onesb", [128, 1], BF16))
        rows_t = ctx.enter_context(nc.sbuf_tensor("rows", [1, 2 * NT], F32))
        rows = rows_t[:]
        amx_t = ctx.enter_context(nc.sbuf_tensor("amx", [128, 3 * NB], F32))
        amx = amx_t[:]
        xss_t = ctx.enter_context(nc.sbuf_tensor("xss", [128, NT], F32))
        xss = xss_t[:]
        car_t = ctx.enter_context(nc.sbuf_tensor("car", [128, 4 * SPER * KC], F32))
        car = car_t[:]
        rsc_t = ctx.enter_context(nc.sbuf_tensor("rsc", [128, 2 * NT], F32))
        rsc = rsc_t[:]
        cst_t = ctx.enter_context(nc.sbuf_tensor("cst", [1, 4], F32))
        cst = cst_t[:]

        psA_t = ctx.enter_context(nc.psum_tensor("psA", [128, 4096], F32))
        psA = psA_t[:]

        def bank(i):
            return psA[:, i * NT:(i + 1) * NT]

        sems = {e: ctx.enter_context(nc.semaphore("sem_" + e)) for e in Sched.ENGS}
        block = ctx.enter_context(nc.Block())

        RB = [f"bank{i}" for i in range(8)]
        R_VEC, R_EWW, R_EYE, R_ROWS, R_AMX, R_XSS = "vec", "eww", "eye", "rows", "amx", "xss"

        def carres(s, w):
            return f"car_{s}_{w}"

        def carcol(which, s):
            base = {"a": 0, "b": SPER, "h": 2 * SPER, "g": 3 * SPER}[which]
            return car[:, (base + s) * KC:(base + s + 1) * KC]

        def bfpages(lo, hi):
            return tuple(f"bf{p}" for p in range(lo // 512, (hi + 511) // 512))

        def reg(i, kc=None):
            if kc is None:
                return bfa[:, i * 4096:(i + 1) * 4096]
            return bfa[:, i * 4096 + kc * NT: i * 4096 + (kc + 1) * NT]

        def regpages(i):
            return bfpages(i * 4096, (i + 1) * 4096)

        def ringres(slot):
            return f"ring{slot}"

        ring_state = {"next": 0}

        def ring_alloc(nslots):
            s0 = ring_state["next"]
            if s0 + nslots > RING_SLOTS:
                s0 = 0
            ring_state["next"] = s0 + nslots
            return s0

        def vcol(name, kc):
            i = VIDX[name] * KC + kc
            return vec[:, i:i + 1]

        def new16():
            return Buf(arena, pool, KC * NT)

        H_STRIDE = NT + 16

        def newH():
            return Buf(arena, pool, KC * H_STRIDE)

        for nm in WNAMES:
            def stg(g, nm=nm):
                g.dma_start(out=wb[nm], in_=wh[nm])
            sch.op("g", stg, reads=(), writes=(f"hbm_{nm}",), dma=True, label=f"stage_{nm}",
                   cost={"fkT": 42000, "fvT": 42000, "wqT": 4000}.get(nm, 11000))

        def ld_vec(y):
            y.dma_start(out=vec, in_=vech)
        sch.op("y", ld_vec, writes=(R_VEC,), dma=True)

        iobuf = Buf(arena, pool, 256)
        ioa_ap = iobuf.ap_full[:, 0:128].bitcast(I32)
        iob_ap = iobuf.ap_full[:, 128:256].bitcast(I32)
        def io1(g):
            g.iota(ioa_ap, pattern=[[1, 128]], channel_multiplier=0)
        def io2(g):
            g.iota(iob_ap, pattern=[[0, 128]], channel_multiplier=1)
        sch.op("g", io1, writes=iobuf.pages)
        sch.op("g", io2, writes=iobuf.pages)
        def mkeye(v):
            v.tensor_tensor(eye, ioa_ap, iob_ap, OP.is_equal)
        sch.op("v", mkeye, reads=iobuf.pages, writes=(R_EYE,))
        iobuf.free()
        def mk1(v):
            v.memset(ones1_t[:], 1.0)
        sch.op("v", mk1, writes=("ones1",))
        def mk2(v):
            v.memset(onesb_t[:], 1.0)
        sch.op("v", mk2, writes=("onesb",))
        def ew1(s):
            s.activation(dvec[:, 0:KC], vec[:, VIDX["time_decay"] * KC:(VIDX["time_decay"] + 1) * KC], AF.Exp)
        sch.op("s", ew1, reads=(R_VEC,), writes=("dvec",))
        def ew2(s):
            s.activation(dvec[:, KC:2 * KC], dvec[:, 0:KC], AF.Exp, scale=-1.0)
        sch.op("s", ew2, reads=("dvec",), writes=("dvec",))
        def ew3(s):
            s.activation(dvec[:, 2 * KC:3 * KC],
                         vec[:, VIDX["time_first"] * KC:(VIDX["time_first"] + 1) * KC], AF.Exp)
        sch.op("s", ew3, reads=(R_VEC,), writes=("dvec",))
        def eww0(v):
            v.memset(eww, 1.0)
        sch.op("v", eww0, writes=(R_EWW,))
        def mkcst(v):
            v.memset(cst[0:1, 0:1], EPS_LN)
        sch.op("v", mkcst, writes=("cst",))
        for kc in range(KC):
            def ewwk(v, kc=kc):
                v.tensor_scalar(eww[:, kc * 128:(kc + 1) * 128], eww[:, kc * 128:(kc + 1) * 128],
                                dvec[:, KC + kc:KC + kc + 1], None, OP.mult)
            sch.op("v", ewwk, reads=("dvec", R_EWW), writes=(R_EWW,))

        def stream_one(nm, mc, kchunks, lb):
            src = wb[nm]
            nsl = max(1, (kchunks * 128) // SLOT)
            s0 = ring_alloc(nsl)
            dst3 = ring[:, s0 * SLOT:s0 * SLOT + kchunks * 128] \
                .rearrange("p (k m) -> p k m", k=kchunks)
            src3 = src[:, mc * 128:(mc + 1) * 128].rearrange("(k p) m -> p k m", p=128)
            rres = tuple(ringres(s0 + i) for i in range(nsl))
            def fn(y, dst3=dst3, src3=src3):
                y.dma_start(out=dst3, in_=src3)
            sch.op("y", fn, reads=(f"hbm_{nm}",), writes=rres, dma=True, label=f"{lb}_w{nm}{mc}", cost=1500)
            return (s0, nsl, rres, mc)

        def slab_block(s0, kc):
            return ring[:, s0 * SLOT + kc * 128: s0 * SLOT + (kc + 1) * 128]

        bank_rr = {"i": 2}

        def next_bank():
            b = bank_rr["i"]
            bank_rr["i"] = 2 + (b - 2 + 1) % 4
            return b

        ln_bp = {"i": 0}

        def next_lnpair():
            p = ((0, 1), (6, 7))[ln_bp["i"] % 2]
            ln_bp["i"] += 1
            return p

        def mm_group(bk, slab, rhs_fn, nK, extra_reads=(), label=""):
            s0, nsl, rres, mc = slab
            def fn(t):
                for kcc in range(nK):
                    t.matmul(bank(bk), slab_block(s0, kcc), rhs_fn(kcc),
                             start=(kcc == 0), stop=(kcc == nK - 1))
            return sch.op("t", fn, reads=rres + tuple(extra_reads), writes=(RB[bk],), label=label, cost=nK * 230)

        def ln_stats(xbuf, lb, bp, r1=5, r2=5):
            b0, b1 = bp
            def cast(s):
                s.activation(reg(r1), xbuf.ap(), AF.Copy)
            sch.op("s", cast, reads=xbuf.pages, writes=regpages(r1), label=lb + "_cast", cost=3700)
            def mm1(t):
                for kc in range(KC):
                    t.matmul(psA[0:1, b0 * NT:b0 * NT + NT], onesb_t[:], reg(r1, kc),
                             start=(kc == 0), stop=(kc == KC - 1))
            sch.op("t", mm1, reads=regpages(r1) + ("onesb",), writes=(RB[b0],), label=lb + "_S1", cost=1800)
            def sq(s):
                s.activation(reg(r2), xbuf.ap(), AF.Square)
            sch.op("s", sq, reads=xbuf.pages, writes=regpages(r2), label=lb + "_sq", cost=3700)
            def mm2(t):
                for kc in range(KC):
                    t.matmul(psA[0:1, b1 * NT:b1 * NT + NT], onesb_t[:], reg(r2, kc),
                             start=(kc == 0), stop=(kc == KC - 1))
            sch.op("t", mm2, reads=regpages(r2) + ("onesb",), writes=(RB[b1],), label=lb + "_S2", cost=1800)
            def rm1(v):
                v.tensor_scalar(rows[0:1, 0:NT], psA[0:1, b0 * NT:b0 * NT + NT], 1.0 / C, None, OP.mult)
            sch.op("v", rm1, reads=(RB[b0],), writes=(R_ROWS,), label=lb + "_m", cost=600)
            def rm2(v):
                v.tensor_tensor(rows[0:1, NT:2 * NT], rows[0:1, 0:NT], rows[0:1, 0:NT], OP.mult)
            sch.op("v", rm2, reads=(R_ROWS,), writes=(R_ROWS,), cost=600)
            def rm3(v):
                v.scalar_tensor_tensor(rows[0:1, NT:2 * NT], psA[0:1, b1 * NT:b1 * NT + NT], 1.0 / C,
                                       rows[0:1, NT:2 * NT], OP.mult, OP.subtract)
            sch.op("v", rm3, reads=(RB[b1], R_ROWS), writes=(R_ROWS,), label=lb + "_var", cost=600)
            def rm4(s):
                s.activation(rows[0:1, NT:2 * NT], rows[0:1, NT:2 * NT], AF.Ln, bias=cst[0:1, 0:1])
            sch.op("s", rm4, reads=(R_ROWS, "cst"), writes=(R_ROWS,), cost=600)
            def rm5(s):
                s.activation(rows[0:1, NT:2 * NT], rows[0:1, NT:2 * NT], AF.Exp, scale=-0.5)
            sch.op("s", rm5, reads=(R_ROWS,), writes=(R_ROWS,), label=lb + "_inv", cost=600)
            def bcm(t):
                t.matmul(bank(b0), ones1_t[:], rows[0:1, 0:NT], start=True, stop=True)
            sch.op("t", bcm, reads=(R_ROWS, "ones1"), writes=(RB[b0],), label=lb + "_bcM", cost=900)
            def bci(t):
                t.matmul(bank(b1), ones1_t[:], rows[0:1, NT:2 * NT], start=True, stop=True)
            sch.op("t", bci, reads=(R_ROWS, "ones1"), writes=(RB[b1],), label=lb + "_bcI", cost=900)

        def dump(name, apv, pages):
            if dev_dump is None or name not in dev_dump:
                return
            shp = list(apv.shape)
            d = nc.dram_tensor("dump_" + name, shp, apv.dtype, kind="ExternalOutput").ap()
            def fn(y):
                y.dma_start(out=d, in_=apv)
            sch.op("y", fn, reads=tuple(pages), writes=(f"dump_{name}",), dma=True, label="dump_" + name, cost=5000)

        def ln_apply(xbuf, dstbuf, lb, bp):
            b0, b1 = bp
            m3 = bank(b0).unsqueeze(1).broadcast_to([128, KC, NT])
            i3 = bank(b1).unsqueeze(1).broadcast_to([128, KC, NT])
            def a1(v):
                v.tensor_tensor(dstbuf.ap(chunks=KC), xbuf.ap(chunks=KC), m3, OP.subtract)
            sch.op("v", a1, reads=xbuf.pages + (RB[b0],), writes=dstbuf.pages, label=lb + "_sub", cost=4500)
            def a2(v):
                v.tensor_tensor(dstbuf.ap(chunks=KC), dstbuf.ap(chunks=KC), i3, OP.mult)
            sch.op("v", a2, reads=dstbuf.pages + (RB[b1],), writes=dstbuf.pages, label=lb + "_mul", cost=4500)

        def tile_prog(ti, s, j):
            lb = f"t{ti}"
            X0A = new16()
            def ldx0(y, s=s, j=j, X0A=X0A):
                y.dma_start(out=X0A.ap(chunks=KC),
                            in_=x0h[s].rearrange("(k p) t -> p k t", p=128)[:, :, j * NT:(j + 1) * NT])
            sch.op("g", ldx0, writes=X0A.pages, dma=True, label=lb + "_ldx0", cost=5600)

            XA = new16()
            bp0 = next_lnpair()
            ln_stats(X0A, lb + "_ln0", bp0)
            yield
            ln_apply(X0A, XA, lb + "_ln0", bp0)
            X0A.free()
            dump(f"xa_{s}_{j}", XA.ap(), XA.pages)

            bp1 = next_lnpair()
            ln_stats(XA, lb + "_ln1", bp1)
            yield
            XH = new16()
            ln_apply(XA, XH, lb + "_ln1", bp1)

            H = newH()
            Hc = H.ap_full.rearrange("p (c n) -> p c n", c=KC)
            if j == 0:
                def h0(v, Hc=Hc):
                    v.memset(Hc[:, :, 0:1], 0.0)
                sch.op("v", h0, writes=H.pages, label=lb + "_h0")
            else:
                hzc2 = {sch.last_writer.get(carres(s, "h"))} - {None}
                def hc(v, Hc=Hc, s=s):
                    v.tensor_copy(Hc[:, :, 0:1], carcol("h", s).unsqueeze(2))
                sch.op("v", hc, reads=(carres(s, "h"),), writes=H.pages, label=lb + "_hc",
                       cost=250, hazard=hzc2)
            for kc in range(KC):
                def aff(v, kc=kc, XH=XH, Hc=Hc):
                    v.tensor_scalar(Hc[:, kc, 1:NT + 1], XH.ap(chunks=KC)[:, kc, :],
                                    vcol("ln1_g", kc), vcol("ln1_b", kc), OP.mult, OP.add)
                sch.op("v", aff, reads=XH.pages + (R_VEC,), writes=H.pages, label=lb + f"_h{kc}", cost=650)
            dump(f"h_{s}_{j}", H.ap_full, H.pages)

            D = XH
            def mkd(v, D=D, Hc=Hc):
                v.tensor_tensor(D.ap(chunks=KC), Hc[:, :, 1:NT + 1], Hc[:, :, 0:NT], OP.subtract)
            sch.op("v", mkd, reads=H.pages, writes=D.pages, label=lb + "_d", cost=4500)
            hzh = {sch.last_writer.get(p) for p in H.pages} - {None}
            def hs(v, Hc=Hc, s=s):
                v.tensor_copy(carcol("h", s), Hc[:, :, NT:NT + 1].squeeze(2))
            sch.op("v", hs, reads=H.pages, writes=(carres(s, "h"),), label=lb + "_hs",
                   cost=250, hazard=hzh)

            for ri, vn in [(2, "tm_k"), (3, "tm_v"), (4, "tm_r")]:
                for kc in range(KC):
                    def mx(v, ri=ri, kc=kc, vn=vn, D=D, Hc=Hc):
                        v.scalar_tensor_tensor(reg(ri, kc), D.ap(chunks=KC)[:, kc, :], vcol(vn, kc),
                                               Hc[:, kc, 0:NT], OP.mult, OP.add)
                    sch.op("v", mx, reads=D.pages + H.pages + (R_VEC,),
                           writes=bfpages(ri * 4096 + kc * NT, ri * 4096 + (kc + 1) * NT),
                           label=lb + f"_mx{ri}{kc}", cost=700)
            D.free()
            yield

            EK, VV, SR = new16(), new16(), new16()
            first_mat = True
            for nm, ri in [("wkT", 2), ("wvT", 3), ("wrT", 4)]:
                if not first_mat:
                    yield
                first_mat = False
                for mc in range(KC):
                    sl = stream_one(nm, mc, KC, lb)
                    bk = next_bank()
                    mm_group(bk, sl, lambda kcc, ri=ri: reg(ri, kcc), KC,
                             extra_reads=regpages(ri), label=lb + f"_{nm}{mc}")
                    if nm == "wkT":
                        def e1(sc, bk=bk, mc=mc, EK=EK):
                            sc.activation(EK.ap(chunks=KC)[:, mc, :], bank(bk), AF.Exp)
                        sch.op("s", e1, reads=(RB[bk],), writes=EK.pages, label=lb + f"_ek{mc}", cost=650)
                    elif nm == "wvT":
                        def e3(sc, bk=bk, mc=mc, VV=VV):
                            sc.activation(VV.ap(chunks=KC)[:, mc, :], bank(bk), AF.Copy)
                        sch.op("s", e3, reads=(RB[bk],), writes=VV.pages, label=lb + f"_v{mc}", cost=650)
                    else:
                        def e4(sc, bk=bk, mc=mc, SR=SR):
                            sc.activation(SR.ap(chunks=KC)[:, mc, :], bank(bk), AF.Copy)
                        sch.op("s", e4, reads=(RB[bk],), writes=SR.pages, label=lb + f"_sr{mc}", cost=650)
            dump(f"ek_{s}_{j}", EK.ap(), EK.pages)
            dump(f"vv_{s}_{j}", VV.ap(), VV.pages)
            dump(f"sr_{s}_{j}", SR.ap(), SR.pages)
            yield

            def scan_into(dst, databuf, cw):
                d3 = dst.ap_full.rearrange("p (c n) -> p c n", c=KC)
                if j == 0:
                    def c0(v, d3=d3):
                        v.memset(d3[:, :, 0:1], 0.0)
                    sch.op("v", c0, writes=dst.pages, label=lb + f"_{cw}c0")
                else:
                    hzc = {sch.last_writer.get(carres(s, cw))} - {None}
                    def c1(v, d3=d3, cw=cw, s=s):
                        v.tensor_copy(d3[:, :, 0:1], carcol(cw, s).unsqueeze(2))
                    sch.op("v", c1, reads=(carres(s, cw),), writes=dst.pages, label=lb + f"_{cw}c",
                           cost=250, hazard=hzc)
                prev_scan = {}
                for q in range(NB):
                    for kc in range(KC):
                        if q == 0:
                            init = (0.0 if j == 0 else carcol(cw, s)[:, kc:kc + 1])
                            ird = () if j == 0 else (carres(s, cw),)
                            hz = None if j == 0 else {sch.last_writer.get(carres(s, cw))}
                        else:
                            init = d3[:, kc, q * 128:q * 128 + 1]
                            ird = dst.pages
                            hz = {prev_scan[kc]}
                        def scn(v, d3=d3, kc=kc, q=q, databuf=databuf, init=init):
                            v.tensor_tensor_scan(d3[:, kc, 1 + q * 128:1 + (q + 1) * 128],
                                                 eww[:, kc * 128:(kc + 1) * 128],
                                                 databuf.ap(chunks=KC)[:, kc, q * 128:(q + 1) * 128],
                                                 init, OP.mult, OP.add)
                        prev_scan[kc] = sch.op("v", scn, reads=databuf.pages + (R_EWW,) + ird,
                                               writes=dst.pages, cost=350,
                                               label=lb + f"_sc{cw}{kc}_{q}", hazard=hz)

            def save_carry(dst, cw):
                d3 = dst.ap_full.rearrange("p (c n) -> p c n", c=KC)
                hz = {sch.last_writer.get(p) for p in dst.pages} - {None}
                def csv(v, d3=d3, cw=cw, s=s):
                    v.tensor_copy(carcol(cw, s), d3[:, :, NT:NT + 1].squeeze(2))
                sch.op("v", csv, reads=dst.pages, writes=(carres(s, cw),), label=lb + f"_csv{cw}",
                       cost=250, hazard=hz)

            HB = Buf(arena, pool, KC * H_STRIDE, view_off=H.off)
            scan_into(HB, EK, "b")
            def mekv(v, EK=EK, VV=VV):
                v.tensor_tensor(VV.ap(), EK.ap(), VV.ap(), OP.mult)
            sch.op("v", mekv, reads=EK.pages + VV.pages, writes=VV.pages, label=lb + "_ekv", cost=4500)
            save_carry(HB, "b")
            Hsh = HB.ap_full.rearrange("p (c n) -> p c n", c=KC)
            hzd = {sch.last_writer.get(p) for p in HB.pages} - {None}
            for kc in range(KC):
                def mden(v, kc=kc, EK=EK, Hsh=Hsh):
                    v.scalar_tensor_tensor(EK.ap(chunks=KC)[:, kc, :], EK.ap(chunks=KC)[:, kc, :],
                                           dvec[:, 2 * KC + kc:2 * KC + kc + 1],
                                           Hsh[:, kc, 0:NT], OP.mult, OP.add)
                sch.op("v", mden, reads=HB.pages + EK.pages + ("dvec",), writes=EK.pages,
                       label=lb + f"_den{kc}", cost=700, hazard=hzd)
            DEN = EK
            dump(f"den_{s}_{j}", DEN.ap(), DEN.pages)
            yield
            scan_into(HB, VV, "a")
            dump(f"A_{s}_{j}", HB.ap_full, HB.pages)
            hzn = {sch.last_writer.get(p) for p in HB.pages} - {None}
            for kc in range(KC):
                def mnum(v, kc=kc, VV=VV, Hsh=Hsh):
                    v.scalar_tensor_tensor(VV.ap(chunks=KC)[:, kc, :], VV.ap(chunks=KC)[:, kc, :],
                                           dvec[:, 2 * KC + kc:2 * KC + kc + 1],
                                           Hsh[:, kc, 0:NT], OP.mult, OP.add)
                sch.op("v", mnum, reads=HB.pages + VV.pages + ("dvec",), writes=VV.pages,
                       label=lb + f"_num{kc}", cost=700, hazard=hzn)
            NUM = VV
            dump(f"num_{s}_{j}", NUM.ap(), NUM.pages)
            save_carry(HB, "a")
            LND = Buf(arena, pool, KC * NT, view_off=H.off)
            def mlnd(sc, LND=LND, DEN=DEN):
                sc.activation(LND.ap(), DEN.ap(), AF.Ln)
            sch.op("s", mlnd, reads=DEN.pages, writes=LND.pages, label=lb + "_lnd", cost=3700)
            RDEN = DEN
            def mrd(sc, RDEN=RDEN, LND=LND):
                sc.activation(RDEN.ap(), LND.ap(), AF.Exp, scale=-1.0)
            sch.op("s", mrd, reads=LND.pages, writes=RDEN.pages, label=lb + "_rden", cost=3700)
            LND.free()
            HB.free()
            def srsig(sc, SR=SR):
                sc.activation(SR.ap(), SR.ap(), AF.Sigmoid)
            sch.op("s", srsig, reads=SR.pages, writes=SR.pages, label=lb + "_srsig", cost=3700)
            def mt1(v, NUM=NUM, RDEN=RDEN):
                v.tensor_tensor(NUM.ap(), NUM.ap(), RDEN.ap(), OP.mult)
            sch.op("v", mt1, reads=NUM.pages + RDEN.pages, writes=NUM.pages, label=lb + "_t1", cost=4500)
            T1 = NUM
            dump(f"wkv_{s}_{j}", T1.ap(), T1.pages)
            def mrw(v, T1=T1, SR=SR):
                v.tensor_tensor(reg(5), T1.ap(), SR.ap(), OP.mult)
            sch.op("v", mrw, reads=T1.pages + SR.pages, writes=regpages(5), label=lb + "_rwkv", cost=4500)
            EK.free()
            VV.free()
            SR.free()
            yield

            X1 = new16()
            for mc in range(KC):
                if mc == 4:
                    yield
                sl = stream_one("woT", mc, KC, lb)
                bk = next_bank()
                mm_group(bk, sl, lambda kcc: reg(5, kcc), KC, extra_reads=regpages(5),
                         label=lb + f"_wo{mc}")
                def mx1(v, bk=bk, mc=mc, X1=X1, XA=XA):
                    v.tensor_tensor(X1.ap(chunks=KC)[:, mc, :], XA.ap(chunks=KC)[:, mc, :],
                                    bank(bk), OP.add)
                sch.op("v", mx1, reads=(RB[bk],) + XA.pages, writes=X1.pages, label=lb + f"_x1{mc}", cost=700)
            XA.free()
            dump(f"x1_{s}_{j}", X1.ap(), X1.pages)
            yield

            bp2 = next_lnpair()
            ln_stats(X1, lb + "_ln2", bp2)
            yield
            XH2 = new16()
            ln_apply(X1, XH2, lb + "_ln2", bp2)
            G = H
            Gc = G.ap_full.rearrange("p (c n) -> p c n", c=KC)
            if j == 0:
                def g0(v, Gc=Gc):
                    v.memset(Gc[:, :, 0:1], 0.0)
                sch.op("v", g0, writes=G.pages, label=lb + "_g0")
            else:
                hzc3 = {sch.last_writer.get(carres(s, "g"))} - {None}
                def gc(v, Gc=Gc, s=s):
                    v.tensor_copy(Gc[:, :, 0:1], carcol("g", s).unsqueeze(2))
                sch.op("v", gc, reads=(carres(s, "g"),), writes=G.pages, label=lb + "_gc",
                       cost=250, hazard=hzc3)
            for kc in range(KC):
                def aff2(v, kc=kc, XH2=XH2, Gc=Gc):
                    v.tensor_scalar(Gc[:, kc, 1:NT + 1], XH2.ap(chunks=KC)[:, kc, :],
                                    vcol("ln2_g", kc), vcol("ln2_b", kc), OP.mult, OP.add)
                sch.op("v", aff2, reads=XH2.pages + (R_VEC,), writes=G.pages, label=lb + f"_g{kc}", cost=650)
            D2 = XH2
            def mkd2(v, D2=D2, Gc=Gc):
                v.tensor_tensor(D2.ap(chunks=KC), Gc[:, :, 1:NT + 1], Gc[:, :, 0:NT], OP.subtract)
            sch.op("v", mkd2, reads=G.pages, writes=D2.pages, label=lb + "_d2", cost=4500)
            hzg = {sch.last_writer.get(p) for p in G.pages} - {None}
            def gs(v, Gc=Gc, s=s):
                v.tensor_copy(carcol("g", s), Gc[:, :, NT:NT + 1].squeeze(2))
            sch.op("v", gs, reads=G.pages, writes=(carres(s, "g"),), label=lb + "_gs",
                   cost=250, hazard=hzg)
            for ri, vn in [(0, "fm_k"), (1, "fm_r")]:
                for kc in range(KC):
                    def mx2(v, ri=ri, kc=kc, vn=vn, D2=D2, Gc=Gc):
                        v.scalar_tensor_tensor(reg(ri, kc), D2.ap(chunks=KC)[:, kc, :], vcol(vn, kc),
                                               Gc[:, kc, 0:NT], OP.mult, OP.add)
                    sch.op("v", mx2, reads=D2.pages + G.pages + (R_VEC,),
                           writes=bfpages(ri * 4096 + kc * NT, ri * 4096 + (kc + 1) * NT),
                           label=lb + f"_mg{ri}{kc}", cost=700)
            D2.free()
            H.free()
            yield

            SIG = new16()
            for mc in range(KC):
                sl = stream_one("frT", mc, KC, lb)
                bk = next_bank()
                mm_group(bk, sl, lambda kcc: reg(1, kcc), KC, extra_reads=regpages(1),
                         label=lb + f"_fr{mc}")
                def sg(sc, bk=bk, mc=mc, SIG=SIG):
                    sc.activation(SIG.ap(chunks=KC)[:, mc, :], bank(bk), AF.Sigmoid)
                sch.op("s", sg, reads=(RB[bk],), writes=SIG.pages, label=lb + f"_sig{mc}", cost=650)

            yield
            kf_lo = 1 * 4096
            for mc in range(FC):
                if mc == 16:
                    yield
                sl = stream_one("fkT", mc, KC, lb)
                bk = next_bank()
                mm_group(bk, sl, lambda kcc: reg(0, kcc), KC, extra_reads=regpages(0),
                         label=lb + f"_fk{mc}")
                rs = rsc[:, (mc % 2) * NT:(mc % 2 + 1) * NT]
                def rl(sc, bk=bk, rs=rs):
                    sc.activation(rs, bank(bk), AF.Relu)
                sch.op("s", rl, reads=(RB[bk],), writes=(f"rsc{mc % 2}",), label=lb + f"_rl{mc}", cost=650)
                dst = bfa[:, kf_lo + mc * NT: kf_lo + (mc + 1) * NT]
                def sq2(sc, rs=rs, dst=dst):
                    sc.activation(dst, rs, AF.Square)
                sch.op("s", sq2, reads=(f"rsc{mc % 2}",),
                       writes=bfpages(kf_lo + mc * NT, kf_lo + (mc + 1) * NT), label=lb + f"_kf{mc}", cost=650)

            yield
            X2 = X1
            for mc in range(KC):
                halves = []
                for half in range(2):
                    s0 = ring_alloc(2)
                    dst3 = ring[:, s0 * SLOT:(s0 + 2) * SLOT].rearrange("p (k m) -> p k m", k=16)
                    src3 = wb["fvT"][half * 2048:(half + 1) * 2048, mc * 128:(mc + 1) * 128] \
                        .rearrange("(k p) m -> p k m", p=128)
                    rres = (ringres(s0), ringres(s0 + 1))
                    def fnl(y, dst3=dst3, src3=src3):
                        y.dma_start(out=dst3, in_=src3)
                    sch.op("y", fnl, reads=("hbm_fvT",), writes=rres, dma=True,
                           label=lb + f"_wfv{mc}_{half}", cost=2800)
                    halves.append((s0, rres))
                bk = next_bank()
                for half, (s0, rres) in enumerate(halves):
                    def fvmm(t, s0=s0, half=half, bk=bk):
                        for kcc in range(16):
                            fcc = half * 16 + kcc
                            t.matmul(bank(bk), slab_block(s0, kcc),
                                     bfa[:, kf_lo + fcc * NT: kf_lo + (fcc + 1) * NT],
                                     start=(fcc == 0), stop=(fcc == FC - 1))
                    sch.op("t", fvmm, reads=rres + bfpages(kf_lo, kf_lo + FC * NT),
                           writes=(RB[bk],), label=lb + f"_fv{mc}_{half}")
                def skv(v, bk=bk, mc=mc, SIG=SIG):
                    v.tensor_tensor(SIG.ap(chunks=KC)[:, mc, :], SIG.ap(chunks=KC)[:, mc, :],
                                    bank(bk), OP.mult)
                sch.op("v", skv, reads=(RB[bk],) + SIG.pages, writes=SIG.pages, label=lb + f"_skv{mc}", cost=700)
                def mx2b(v, mc=mc, X2=X2, SIG=SIG):
                    v.tensor_tensor(X2.ap(chunks=KC)[:, mc, :], X2.ap(chunks=KC)[:, mc, :],
                                    SIG.ap(chunks=KC)[:, mc, :], OP.add)
                sch.op("v", mx2b, reads=X2.pages + SIG.pages, writes=X2.pages, label=lb + f"_x2{mc}", cost=700)
            SIG.free()
            yield

            X0D = new16()
            def ldx0d(y, s=s, j=j, X0D=X0D):
                y.dma_start(out=X0D.ap(chunks=KC),
                            in_=x0h[s].rearrange("(k p) t -> p k t", p=128)[:, :, j * NT:(j + 1) * NT])
            sch.op("g", ldx0d, writes=X0D.pages, dma=True, label=lb + "_ldx0d", cost=5600)
            XX = X0D
            def mX(v, XX=XX, X2=X2):
                v.tensor_tensor(XX.ap(), XX.ap(), X2.ap(), OP.add)
            sch.op("v", mX, reads=XX.pages + X2.pages, writes=XX.pages, label=lb + "_X", cost=4500)
            dump(f"X_{s}_{j}", XX.ap(), XX.pages)
            X2.free()

            SQ3 = new16()
            def sq3(sc, XX=XX, SQ3=SQ3):
                sc.activation(SQ3.ap(), XX.ap(), AF.Square)
            sch.op("s", sq3, reads=XX.pages, writes=SQ3.pages, label=lb + "_sq3", cost=3700)
            bp3 = next_lnpair()
            qp = (6, 7) if bp3 == (0, 1) else (0, 1)
            b0, b1 = bp3
            def c3(sc, XX=XX):
                sc.activation(reg(3), XX.ap(), AF.Copy)
            sch.op("s", c3, reads=XX.pages, writes=regpages(3), label=lb + "_c3", cost=3700)
            def c3b(v, SQ3=SQ3):
                v.tensor_copy(reg(4), SQ3.ap())
            sch.op("v", c3b, reads=SQ3.pages, writes=regpages(4), label=lb + "_c3b", cost=2400)
            def mm1(t, b0=b0):
                for kc in range(KC):
                    t.matmul(psA[0:1, b0 * NT:b0 * NT + NT], onesb_t[:], reg(3, kc),
                             start=(kc == 0), stop=(kc == KC - 1))
            sch.op("t", mm1, reads=regpages(3) + ("onesb",), writes=(RB[b0],), label=lb + "_l3S1", cost=1800)
            def mm2(t, b1=b1):
                for kc in range(KC):
                    t.matmul(psA[0:1, b1 * NT:b1 * NT + NT], onesb_t[:], reg(4, kc),
                             start=(kc == 0), stop=(kc == KC - 1))
            sch.op("t", mm2, reads=regpages(4) + ("onesb",), writes=(RB[b1],), label=lb + "_l3S2", cost=1800)
            def rm1(v, b0=b0):
                v.tensor_scalar(rows[0:1, 0:NT], psA[0:1, b0 * NT:b0 * NT + NT], 1.0 / C, None, OP.mult)
            sch.op("v", rm1, reads=(RB[b0],), writes=(R_ROWS,))
            def rm2(v):
                v.tensor_tensor(rows[0:1, NT:2 * NT], rows[0:1, 0:NT], rows[0:1, 0:NT], OP.mult)
            sch.op("v", rm2, reads=(R_ROWS,), writes=(R_ROWS,))
            def rm3(v, b1=b1):
                v.scalar_tensor_tensor(rows[0:1, NT:2 * NT], psA[0:1, b1 * NT:b1 * NT + NT], 1.0 / C,
                                       rows[0:1, NT:2 * NT], OP.mult, OP.subtract)
            sch.op("v", rm3, reads=(RB[b1], R_ROWS), writes=(R_ROWS,))
            def rm4(sc):
                sc.activation(rows[0:1, NT:2 * NT], rows[0:1, NT:2 * NT], AF.Ln, bias=cst[0:1, 0:1])
            sch.op("s", rm4, reads=(R_ROWS, "cst"), writes=(R_ROWS,))
            def rm5(sc):
                sc.activation(rows[0:1, NT:2 * NT], rows[0:1, NT:2 * NT], AF.Exp, scale=-0.5)
            sch.op("s", rm5, reads=(R_ROWS,), writes=(R_ROWS,))
            def bcm(t, b0=b0):
                t.matmul(bank(b0), ones1_t[:], rows[0:1, 0:NT], start=True, stop=True)
            sch.op("t", bcm, reads=(R_ROWS, "ones1"), writes=(RB[b0],))
            def bci(t, b1=b1):
                t.matmul(bank(b1), ones1_t[:], rows[0:1, NT:2 * NT], start=True, stop=True)
            sch.op("t", bci, reads=(R_ROWS, "ones1"), writes=(RB[b1],))
            yield

            XF = new16()
            ln_apply(XX, XF, lb + "_ln3", bp3)
            dump(f"xf_{s}_{j}", XF.ap(), XF.pages)
            def sqf(sc, XF=XF, SQ3=SQ3):
                sc.activation(SQ3.ap(), XF.ap(), AF.Square)
            sch.op("s", sqf, reads=XF.pages, writes=SQ3.pages, label=lb + "_sqf", cost=3700)

            def t1f(v, SQ3=SQ3):
                v.tensor_tensor(SQ3.ap(chunks=KC)[:, 0:4, :], SQ3.ap(chunks=KC)[:, 0:4, :],
                                SQ3.ap(chunks=KC)[:, 4:8, :], OP.max)
            sch.op("v", t1f, reads=SQ3.pages, writes=SQ3.pages, label=lb + "_tr1", cost=2400)
            def t2f(v, SQ3=SQ3):
                v.tensor_tensor(SQ3.ap(chunks=KC)[:, 4:6, :], SQ3.ap(chunks=KC)[:, 0:2, :],
                                SQ3.ap(chunks=KC)[:, 2:4, :], OP.max)
            sch.op("v", t2f, reads=SQ3.pages, writes=SQ3.pages)
            def t3f(v, SQ3=SQ3):
                v.tensor_tensor(SQ3.ap(chunks=KC)[:, 6:7, :], SQ3.ap(chunks=KC)[:, 4:5, :],
                                SQ3.ap(chunks=KC)[:, 5:6, :], OP.max)
            sch.op("v", t3f, reads=SQ3.pages, writes=SQ3.pages)
            for bi in range(NB):
                def trp(t, bi=bi, SQ3=SQ3):
                    t.transpose(psA[:, qp[0] * NT + bi * 128:qp[0] * NT + (bi + 1) * 128],
                                SQ3.ap(chunks=KC)[:, 6, bi * 128:(bi + 1) * 128], eye)
                sch.op("t", trp, reads=SQ3.pages + (R_EYE,), writes=(RB[qp[0]],), label=lb + f"_trp{bi}")
            def redm(v):
                v.tensor_reduce(amx[:, 0:NB], bank(qp[0]).rearrange("p (a b) -> p a b", a=NB),
                                AX, OP.max)
            sch.op("v", redm, reads=(RB[qp[0]],), writes=(R_AMX,), label=lb + "_red", cost=700)
            SQ3.free()
            def am1(sc):
                sc.activation(amx[:, NB:2 * NB], amx[:, 0:NB], AF.Ln)
            sch.op("s", am1, reads=(R_AMX,), writes=(R_AMX,))
            def am2(sc):
                sc.activation(amx[:, NB:2 * NB], amx[:, NB:2 * NB], AF.Exp, scale=0.5)
            sch.op("s", am2, reads=(R_AMX,), writes=(R_AMX,))
            def am3(v):
                v.tensor_scalar(amx[:, NB:2 * NB], amx[:, NB:2 * NB], EPS_Q, 1.0 / 127.0,
                                OP.max, OP.mult)
            am3_id = sch.op("v", am3, reads=(R_AMX,), writes=(R_AMX,), label=lb + "_xs", cost=300)
            def am4(v):
                v.reciprocal(amx[:, 2 * NB:3 * NB], amx[:, NB:2 * NB])
            sch.op("v", am4, reads=(R_AMX,), writes=(R_AMX,), label=lb + "_rq",
                   cost=300, hazard={am3_id})
            dump(f"amx_{s}_{j}", amx, (R_AMX,))
            for bi in range(NB):
                def tx(t, bi=bi):
                    t.transpose(psA[0:1, qp[1] * NT + bi * 128:qp[1] * NT + (bi + 1) * 128],
                                amx[:, NB + bi:NB + bi + 1], eye)
                sch.op("t", tx, reads=(R_AMX, R_EYE), writes=(RB[qp[1]],), label=lb + f"_tx{bi}")
            for bi in range(NB):
                def tr(t, bi=bi):
                    t.transpose(psA[0:1, qp[0] * NT + bi * 128:qp[0] * NT + (bi + 1) * 128],
                                amx[:, 2 * NB + bi:2 * NB + bi + 1], eye)
                sch.op("t", tr, reads=(R_AMX, R_EYE), writes=(RB[qp[0]],), label=lb + f"_tr{bi}")
            def evx(sc):
                sc.activation(rows[0:1, 0:NT], psA[0:1, qp[1] * NT:qp[1] * NT + NT], AF.Copy)
            sch.op("s", evx, reads=(RB[qp[1]],), writes=(R_ROWS,), label=lb + "_evxs")
            def evr(sc):
                sc.activation(rows[0:1, NT:2 * NT], psA[0:1, qp[0] * NT:qp[0] * NT + NT], AF.Copy)
            sch.op("s", evr, reads=(RB[qp[0]],), writes=(R_ROWS,), label=lb + "_evr")
            dump(f"rows_{s}_{j}", rows, (R_ROWS,))
            def bcr(t, b0=b0):
                t.matmul(bank(b0), ones1_t[:], rows[0:1, NT:2 * NT], start=True, stop=True)
            sch.op("t", bcr, reads=(R_ROWS, "ones1"), writes=(RB[b0],), label=lb + "_bcr")
            def bcx(t, b1=b1):
                t.matmul(bank(b1), ones1_t[:], rows[0:1, 0:NT], start=True, stop=True)
            sch.op("t", bcx, reads=(R_ROWS, "ones1"), writes=(RB[b1],), label=lb + "_bcx")
            def evxs(sc, b1=b1):
                sc.activation(xss, bank(b1), AF.Copy)
            sch.op("s", evxs, reads=(RB[b1],), writes=(R_XSS,), label=lb + "_evb")
            dump(f"xss_{s}_{j}", xss, (R_XSS,))

            r3 = bank(b0).unsqueeze(1).broadcast_to([128, KC, NT])
            def q1(v, XF=XF):
                v.tensor_tensor(XF.ap(chunks=KC), XF.ap(chunks=KC), r3, OP.mult)
            sch.op("v", q1, reads=XF.pages + (RB[b0],), writes=XF.pages, label=lb + "_q1", cost=4500)
            def q2(v, XF=XF):
                v.tensor_scalar(reg(2), XF.ap(), RND_C, RND_C, OP.add, OP.subtract)
            sch.op("v", q2, reads=XF.pages, writes=regpages(2), label=lb + "_q2", cost=2400)
            XF.free()

            yield
            OUTB = new16()
            for mc in range(KC):
                sl = stream_one("wqT", mc, KC, lb)
                bk = next_bank()
                mm_group(bk, sl, lambda kcc: reg(2, kcc), KC, extra_reads=regpages(2),
                         label=lb + f"_wq{mc}")
                def dq(v, bk=bk, mc=mc, OUTB=OUTB):
                    v.scalar_tensor_tensor(OUTB.ap(chunks=KC)[:, mc, :], bank(bk), vcol("wscale", mc),
                                           xss, OP.mult, OP.mult)
                sch.op("v", dq, reads=(RB[bk], R_XSS, R_VEC), writes=OUTB.pages, label=lb + f"_dq{mc}", cost=700)
                def fin(v, mc=mc, OUTB=OUTB, XX=XX):
                    v.scalar_tensor_tensor(OUTB.ap(chunks=KC)[:, mc, :], OUTB.ap(chunks=KC)[:, mc, :],
                                           vcol("bias", mc), XX.ap(chunks=KC)[:, mc, :], OP.add, OP.add)
                sch.op("v", fin, reads=OUTB.pages + XX.pages + (R_VEC,), writes=OUTB.pages,
                       label=lb + f"_fin{mc}", cost=700)
            XX.free()
            def store(y, s=s, j=j, OUTB=OUTB):
                y.dma_start(out=outh[s].rearrange("(k p) t -> p k t", p=128)[:, :, j * NT:(j + 1) * NT],
                            in_=OUTB.ap(chunks=KC))
            sch.op("g", store, reads=OUTB.pages, writes=(f"out_{s}_{j}",), dma=True, label=lb + "_store", cost=5600)
            dump(f"car_{s}_{j}", car, (carres(s, "a"), carres(s, "b"), carres(s, "h"), carres(s, "g")))
            OUTB.free()

        n_tiles = len(tiles)
        gens = {}

        def step(r, k=1):
            for _ in range(k):
                next(gens[r], None)

        for r in range(n_tiles + 1):
            if r < n_tiles:
                gens[r] = tile_prog(r, *tiles[r])
            p, q = r - 1, r
            if p >= 0:
                step(p, 2)
            if q < n_tiles:
                step(q)
            if p >= 0:
                step(p)
            if q < n_tiles:
                step(q)
            if p >= 0:
                step(p)
            if q < n_tiles:
                step(q)
            if p >= 0:
                step(p)
            if q < n_tiles:
                step(q)
            if p >= 0:
                step(p)
            if q < n_tiles:
                step(q, 2)
            if p >= 0:
                step(p, 2)
            if q < n_tiles:
                step(q)
            if p >= 0:
                step(p)
            if q < n_tiles:
                step(q)
            if p >= 0:
                step(p)
                for _ in gens[p]:
                    pass
                del gens[p]

        class IncWrap:
            def __init__(self, h):
                self._h = h
                self.last = None

            def __getattr__(self, k):
                a = getattr(self._h, k)
                if callable(a):
                    def call(*args, **kw):
                        r = a(*args, **kw)
                        if hasattr(r, "then_inc"):
                            self.last = r
                        return r
                    return call
                return a

        import os
        use_scopes = bool(os.environ.get("K_SCOPES"))

        order = sch.schedule()
        sepname = [0]
        vorder = order["v"]
        fixed = []
        for idx, oid in enumerate(vorder):
            o = sch.ops[oid]
            if o.hazard and idx > 0 and vorder[idx - 1] in o.hazard:
                def mksep(v):
                    v.memset(cst[0:1, 2:3], 0.0)
                fixed.append(Op(-1, "v", mksep, 1, "sep", 100, set(), None))
            fixed.append(o)
        order["v"] = fixed
        for e in Sched.ENGS:
            if e != "v":
                order[e] = [sch.ops[oid] for oid in order[e]]
        cum = {}
        cnt = {e: 0 for e in Sched.ENGS}
        for e in Sched.ENGS:
            for o in order[e]:
                cnt[e] += o.inc
                if o.oid >= 0:
                    cum[o.oid] = cnt[e]
        sch.cnt = cnt

        def emit(eng, handle):
            w = IncWrap(handle)
            hiw = {f: 0 for f in Sched.ENGS}
            cnt2 = 0
            for o in order[eng]:
                for d in (o.deps or ()):
                    de = sch.ops[d].eng
                    if de == eng:
                        continue
                    dv = cum[d]
                    if dv > hiw[de]:
                        hiw[de] = dv
                        handle.wait_ge(sems[de], dv)
                w.last = None
                if use_scopes:
                    with nc.named_scope(o.label or "anon"):
                        o.fn(w)
                else:
                    o.fn(w)
                assert w.last is not None, f"op issued no instruction: {eng} {o.label}"
                w.last.then_inc(sems[eng], o.inc)

        @block.vector
        def _(v):
            emit("v", v)

        @block.scalar
        def _(s_):
            emit("s", s_)

        @block.tensor
        def _(t_):
            emit("t", t_)

        @block.gpsimd
        def _(g_):
            emit("g", g_)

        @block.sync
        def _(y_):
            emit("y", y_)
            y_.wait_ge(sems["y"], sch.cnt["y"])

    return nc


def _host_prep(inputs):
    x = np.asarray(inputs["L0_attn_x"], dtype=np.float32)
    xT = np.ascontiguousarray(np.transpose(x, (0, 2, 1)))
    tr = lambda k: np.ascontiguousarray(np.asarray(inputs[k], np.float32).T)
    shared = dict(
        wkT=tr("L0_attn_Wk"), wvT=tr("L0_attn_Wv"), wrT=tr("L0_attn_Wr"),
        woT=tr("L0_attn_Wo"), fkT=tr("L0_attn_Fk"), frT=tr("L0_attn_Fr"),
        fvT=tr("L0_attn_Fv"),
        wqT=np.ascontiguousarray(np.asarray(inputs["L0_ffn_weight_q"]).T),
    )
    vecmap = {
        "ln1_g": "L0_attn_ln1_g", "ln1_b": "L0_attn_ln1_b",
        "ln2_g": "L0_attn_ln2_g", "ln2_b": "L0_attn_ln2_b",
        "tm_k": "L0_attn_tm_k", "tm_v": "L0_attn_tm_v", "tm_r": "L0_attn_tm_r",
        "time_decay": "L0_attn_time_decay", "time_first": "L0_attn_time_first",
        "fm_k": "L0_attn_fm_k", "fm_r": "L0_attn_fm_r",
        "wscale": "L0_ffn_weight_scale", "bias": "L0_ffn_bias",
    }
    cols = [np.asarray(inputs[vecmap[nm]], np.float32).reshape(KC, 128).T for nm in VECS]
    shared["vecs"] = np.ascontiguousarray(np.concatenate(cols, axis=1))
    return xT, shared


def kernel(**inputs):
    xT, shared = _host_prep(inputs)
    nc = build_core_program()
    in_maps = []
    for ci in range(NCORES):
        m = dict(shared)
        m["x0"] = np.ascontiguousarray(xT[ci * SPER:(ci + 1) * SPER])
        in_maps.append(m)
    res = run_bass_kernel_spmd(nc, in_maps, core_ids=list(range(NCORES)))
    outs = [np.transpose(r["out"], (0, 2, 1)) for r in res.results]
    return np.ascontiguousarray(np.concatenate(outs, axis=0))
